# version 07
# baseline (speedup 1.0000x reference)
import numpy as np
import concourse.bass as bass
import concourse.mybir as mybir
from concourse.bass_utils import run_bass_kernel_spmd
from concourse.alu_op_type import AluOpType

F32 = mybir.dt.float32
U16 = mybir.dt.uint16
BF16 = mybir.dt.bfloat16
AF = mybir.ActivationFunctionType
AX = mybir.AxisListType

NF4 = np.array([-1.0, -0.6961928009986877, -0.5250730514526367, -0.39491748809814453,
                -0.28444138169288635, -0.18477343022823334, -0.09105003625154495, 0.0,
                0.07958029955625534, 0.16093020141124725, 0.2461123019456863,
                0.33791524171829224, 0.44070982933044434, 0.5626170039176941,
                0.7229568362236023, 1.0], dtype=np.float32)
DV = (NF4[1:] - NF4[:-1]).astype(np.float32)
HD = [float(np.float32(DV[v - 1]) * np.float32(0.5)) for v in range(1, 16)]

D = 4096
FF = 11008
SC = 1024
H = 512
NIC = 32
NFC = 86
NOT = 8
NG = 11
EPS = 1e-5

G_OCS = [(0, 2048), (2048, 2048), (4096, 2048), (6144, 2048), (8192, 2048), (10240, 768)]
D_OCS = [(0, 2048), (2048, 2048)]
GU_TILES = []
for _oci, (_oo, _ow) in enumerate(G_OCS):
    for _kc in range(NIC):
        GU_TILES.append(("g", _kc, _oo, _ow))
        GU_TILES.append(("u", _kc, _oo, _ow))
D_TILES = []
for (_oo, _ow) in D_OCS:
    for _fc in range(NFC):
        D_TILES.append(("d", _fc, _oo, _ow))
DQ_TILES = GU_TILES + D_TILES


def build_nc(debug=False, phases="full"):
    nc = bass.Bass("TRN2", target_bir_lowering=False, detect_race_conditions=False)
    dbg_r = dbg_x3 = None
    if debug:
        dbg_r = nc.dram_tensor("dbg_r", [SC, D], F32, kind="ExternalOutput").ap()
        dbg_x3 = nc.dram_tensor("dbg_x3", [4, 128, SC], F32, kind="ExternalOutput").ap()
        dbg_w = nc.dram_tensor("dbg_w", [8, 128, 128], F32, kind="ExternalOutput").ap()

    x_h = nc.dram_tensor("x", [SC, D], F32, kind="ExternalInput").ap()
    cod_src = {
        "g": nc.dram_tensor("g16", [FF, D], U16, kind="ExternalInput").ap(),
        "u": nc.dram_tensor("u16", [FF, D], U16, kind="ExternalInput").ap(),
        "d": nc.dram_tensor("d16", [D, FF], U16, kind="ExternalInput").ap(),
    }
    am_src = {
        "g": nc.dram_tensor("amxg", [NIC, 128, FF], F32, kind="ExternalInput").ap(),
        "u": nc.dram_tensor("amxu", [NIC, 128, FF], F32, kind="ExternalInput").ap(),
        "d": nc.dram_tensor("amxd", [NFC, 128, D], F32, kind="ExternalInput").ap(),
    }
    ga_h = nc.dram_tensor("ga", [D, 16], F32, kind="ExternalInput").ap()
    gb_h = nc.dram_tensor("gb", [16, FF], F32, kind="ExternalInput").ap()
    ua_h = nc.dram_tensor("ua", [D, 16], F32, kind="ExternalInput").ap()
    ub_h = nc.dram_tensor("ub", [16, FF], F32, kind="ExternalInput").ap()
    da_h = nc.dram_tensor("da", [FF, 16], F32, kind="ExternalInput").ap()
    db_h = nc.dram_tensor("db", [16, D], F32, kind="ExternalInput").ap()
    id_h = nc.dram_tensor("ident", [128, 128], F32, kind="ExternalInput").ap()
    out_h = nc.dram_tensor("out", [SC, D], F32, kind="ExternalOutput").ap()

    r_scr = nc.dram_tensor("r_scr", [SC, D], F32, kind="Internal").ap()
    x3_scr = nc.dram_tensor("x3_scr", [NFC, 128, SC], F32, kind="Internal").ap()
    wt_dst = {
        "g": nc.dram_tensor("wtg", [NFC, 128, NIC * 128], F32, kind="Internal").ap(),
        "u": nc.dram_tensor("wtu", [NFC, 128, NIC * 128], F32, kind="Internal").ap(),
        "d": nc.dram_tensor("wtd", [NFC, 128, NIC * 128], F32, kind="Internal").ap(),
    }

    V = {}
    CNT = {}

    def inc(sem_name, by):
        CNT[sem_name] = CNT.get(sem_name, 0) + by
        return CNT[sem_name]

    def cur(sem_name):
        return CNT.get(sem_name, 0)

    ops = {"sync": [], "tensor": [], "vector": [], "scalar": [], "gpsimd": []}

    def emit(eng, fn):
        ops[eng].append(fn)

    from contextlib import ExitStack
    with ExitStack() as ctx:
        e = ctx.enter_context
        big = e(nc.sbuf_tensor("big", [128, 16384], F32))
        xta = e(nc.sbuf_tensor("xta", [128, D], F32))
        xtb = e(nc.sbuf_tensor("xtb", [128, D], F32))
        cbuf = e(nc.sbuf_tensor("cbuf", [128, 2 * 2048], U16))
        zbuf = e(nc.sbuf_tensor("zbuf", [128, 3 * 2048], BF16))
        wacc = e(nc.sbuf_tensor("wacc", [128, 2 * 2048], F32))
        amb = e(nc.sbuf_tensor("amb", [128, 2 * 2048], F32))
        wring = e(nc.sbuf_tensor("wring", [128, 2 * 4096], F32))
        x3t = e(nc.sbuf_tensor("x3t", [128, 2 * 512], F32))
        gas = e(nc.sbuf_tensor("gas", [128, NIC * 16], F32))
        uas = e(nc.sbuf_tensor("uas", [128, NIC * 16], F32))
        das = e(nc.sbuf_tensor("das", [128, NFC * 16], F32))
        gbs = e(nc.sbuf_tensor("gbs", [16, 2 * 128], F32))
        ubs = e(nc.sbuf_tensor("ubs", [16, 2 * 128], F32))
        dbs = e(nc.sbuf_tensor("dbs", [16, 2 * 512], F32))
        tgT = bass.AP(dbs, 0, [[1024, 16], [1, 512]])
        tuT = bass.AP(dbs, 512, [[1024, 16], [1, 512]])
        tdT = e(nc.sbuf_tensor("tdT", [16, 1024], F32))
        ident = e(nc.sbuf_tensor("ident_sb", [128, 128], F32))
        st1 = e(nc.sbuf_tensor("st1", [128, 1], F32))
        st2 = e(nc.sbuf_tensor("st2", [128, 1], F32))
        stm = st1[:]
        stv = e(nc.sbuf_tensor("stv", [128, 1], F32))
        strs = e(nc.sbuf_tensor("strs", [128, 1], F32))
        stn = e(nc.sbuf_tensor("stn", [128, 1], F32))
        kconst = e(nc.sbuf_tensor("kconst", [128, 17], F32))
        pA = e(nc.psum_tensor("pA", [128, 512], F32))
        pB = e(nc.psum_tensor("pB", [128, 512], F32))
        pC = e(nc.psum_tensor("pC", [128, 512], F32))
        pD = e(nc.psum_tensor("pD", [128, 512], F32))
        pT = e(nc.psum_tensor("pT", [128, 512], F32))
        pt3 = e(nc.psum_tensor("pt3", [128, 512], F32))
        pamA = e(nc.psum_tensor("pamA", [128, 512], F32))
        pamB = e(nc.psum_tensor("pamB", [128, 512], F32))
        s_lm = e(nc.semaphore("s_lm"))
        s_lx = e(nc.semaphore("s_lx"))
        s_ldc = [e(nc.semaphore("s_ldc0")), e(nc.semaphore("s_ldc1"))]
        s_lda = [e(nc.semaphore("s_lda0")), e(nc.semaphore("s_lda1"))]
        s_ldw = e(nc.semaphore("s_ldw"))
        s_ldg = e(nc.semaphore("s_ldg"))
        s_lg = [e(nc.semaphore("s_lg0")), e(nc.semaphore("s_lg1"))]
        s_lr = e(nc.semaphore("s_lr"))
        s_sr = e(nc.semaphore("s_sr"))
        s_sx = [e(nc.semaphore("s_sx0")), e(nc.semaphore("s_sx1"))]
        s_stw = [e(nc.semaphore("s_stw0")), e(nc.semaphore("s_stw1"))]
        s_so = e(nc.semaphore("s_so"))
        s_stb = e(nc.semaphore("s_stb"))
        s_pl = e(nc.semaphore("s_pl"))
        s_lam = [e(nc.semaphore("s_lam0")), e(nc.semaphore("s_lam1"))]
        s_act = e(nc.semaphore("s_act"))
        s_dve = e(nc.semaphore("s_dve"))
        s_pe = e(nc.semaphore("s_pe"))
        s_st = e(nc.semaphore("s_st"))
        blk = e(nc.Block())
        def bigv(ic, w=512):
            return bass.AP(big, ic * 512, [[16384, 128], [1, w]])

        def accv(sb, off, w):
            return bass.AP(big, sb * 4096 + off, [[16384, 128], [1, w]])

        def lora_loads():
            def f(sync):
                sync.dma_start(out=ident[:], in_=id_h[:]).then_inc(s_lm, 16)
                sync.dma_start(out=bass.AP(gas, 0, [[NIC * 16, 128], [16, NIC], [1, 16]]),
                               in_=ga_h.rearrange("(c p) r -> p c r", p=128)).then_inc(s_lm, 16)
                sync.dma_start(out=bass.AP(uas, 0, [[NIC * 16, 128], [16, NIC], [1, 16]]),
                               in_=ua_h.rearrange("(c p) r -> p c r", p=128)).then_inc(s_lm, 16)
                sync.dma_start(out=bass.AP(das, 0, [[NFC * 16, 128], [16, NFC], [1, 16]]),
                               in_=da_h.rearrange("(c p) r -> p c r", p=128)).then_inc(s_lm, 16)
            emit("sync", f)
            V["ld_misc"] = inc("lm", 64)

        lora_loads()

        def const_init():
            def f(vector):
                vector.memset(kconst[:, 0:1], EPS)
                for v in range(1, 15):
                    vector.memset(kconst[:, v:v + 1], float(-(v - 0.5)))
                vector.memset(kconst[:, 15:16], float(-(15 - 0.5)))
                vector.memset(kconst[:, 16:17], 0.0)
                vector.drain().then_inc(s_dve, 1)
            emit("vector", f)
            V["consts"] = inc("dve", 1)
        const_init()

        def p0(h):
            pe_guard = cur("pe")
            first_evac_done = [False]
            for st in range(4):
                row0 = h * H + st * 128
                prev_r_store = V.get("last_r_store", 0)
                def fl(sync, row0=row0, prev_r_store=prev_r_store):
                    if prev_r_store:
                        sync.wait_ge(s_sr, prev_r_store)
                    sync.dma_start(out=xta[:], in_=x_h[row0:row0 + 128, :]).then_inc(s_lx, 16)
                ldx_t = inc("lx", 16)
                emit("sync", fl)

                prev_xn = V.get("p0_prev_xn", 0)
                wacc_free = cur("stb")
                def fv(vector, ldx_t=ldx_t, prev_xn=prev_xn, wacc_free=wacc_free):
                    vector.wait_ge(s_lx, ldx_t)
                    if prev_xn:
                        vector.wait_ge(s_act, prev_xn)
                    if wacc_free:
                        vector.wait_ge(s_stb, wacc_free)
                    zz = bass.AP(wacc, 0, [[2 * 2048, 128], [1, D]])
                    vector.reduce_sum(st1[:], xta[:], axis=AX.X)
                    vector.tensor_tensor(out=zz, in0=xta[:], in1=xta[:], op=AluOpType.mult)
                    vector.reduce_sum(st2[:], zz, axis=AX.X)
                    vector.drain()
                    vector.tensor_scalar(out=stm, in0=st1[:], scalar1=1.0 / D,
                                         scalar2=None, op0=AluOpType.mult)
                    vector.tensor_scalar(out=st2[:], in0=st2[:], scalar1=1.0 / D,
                                         scalar2=None, op0=AluOpType.mult)
                    vector.drain()
                    vector.tensor_tensor(out=stv[:], in0=stm, in1=stm, op=AluOpType.mult)
                    vector.drain()
                    vector.tensor_tensor(out=stv[:], in0=st2[:], in1=stv[:],
                                         op=AluOpType.subtract)
                    vector.drain().then_inc(s_dve, 1)
                dve_stats = inc("dve", 1)
                emit("vector", fv)

                def fa(scalar, dve_stats=dve_stats):
                    scalar.wait_ge(s_dve, dve_stats)
                    scalar.wait_ge(s_dve, V["consts"])
                    scalar.activation(out=strs[:], in_=stv[:], func=AF.Sqrt,
                                      bias=kconst[0:128, 0:1], scale=1.0)
                    scalar.drain().then_inc(s_act, 1)
                act_rstd = inc("act", 1)
                emit("scalar", fa)

                def fv2(vector, act_rstd=act_rstd):
                    vector.wait_ge(s_act, act_rstd)
                    vector.reciprocal(out=strs[:], in_=strs[:])
                    vector.drain()
                    vector.tensor_scalar(out=stn[:], in0=stm, scalar1=strs[0:128, 0:1],
                                         scalar2=-1.0, op0=AluOpType.mult,
                                         op1=AluOpType.mult)
                    vector.drain().then_inc(s_dve, 1)
                dve_stn = inc("dve", 1)
                emit("vector", fv2)

                prev_tr = V.get("p0_prev_tr", 0)
                def fa2(scalar, dve_stn=dve_stn, prev_tr=prev_tr):
                    scalar.wait_ge(s_dve, dve_stn)
                    if prev_tr:
                        scalar.wait_ge(s_pe, prev_tr)
                    scalar.activation(out=xtb[:], in_=xta[:], func=AF.Identity,
                                      bias=stn[0:128, 0:1], scale=strs[0:128, 0:1])
                    scalar.drain().then_inc(s_act, 1)
                act_xn = inc("act", 1)
                V["p0_prev_xn"] = act_xn
                emit("scalar", fa2)

                def fv3(vector, act_xn=act_xn):
                    vector.wait_ge(s_act, act_xn)
                    vector.scalar_tensor_tensor(out=xta[:], in0=xtb[:], scalar=2.0,
                                                in1=xta[:], op0=AluOpType.mult,
                                                op1=AluOpType.add)
                    vector.drain().then_inc(s_dve, 1)
                dve_r = inc("dve", 1)
                emit("vector", fv3)

                def fp(gpsimd, row0=row0, dve_r=dve_r):
                    gpsimd.wait_ge(s_dve, dve_r)
                    gpsimd.dma_start(out=r_scr[row0:row0 + 128, :], in_=xta[:]).then_inc(s_sr, 16)
                V["last_r_store"] = inc("sr", 16)
                emit("gpsimd", fp)
                V[f"r_store_{h}"] = V["last_r_store"]

                for icb in range(8):
                    pT_guard = V.get("pT_guard", 0)
                    ld_misc = V["ld_misc"]
                    def ft(tensor, icb=icb, act_xn=act_xn, pT_guard=pT_guard, ld_misc=ld_misc):
                        tensor.wait_ge(s_act, act_xn)
                        tensor.wait_ge(s_lm, ld_misc)
                        if pT_guard:
                            tensor.wait_ge(s_act, pT_guard)
                        for j in range(4):
                            ic = icb * 4 + j
                            tensor.transpose(bass.AP(pT, j * 128, [[512, 128], [1, 128]]),
                                             xtb[:, ic * 128:(ic + 1) * 128],
                                             ident[:]).then_inc(s_pe, 1)
                    pe_tr = inc("pe", 4)
                    emit("tensor", ft)

                    need_pe_guard = pe_guard if not first_evac_done[0] else 0
                    first_evac_done[0] = True
                    def fa3(scalar, icb=icb, st=st, pe_tr=pe_tr, need_pe_guard=need_pe_guard):
                        scalar.wait_ge(s_pe, pe_tr)
                        if need_pe_guard:
                            scalar.wait_ge(s_pe, need_pe_guard)
                        dst = bass.AP(big, (icb * 4) * 512 + st * 128,
                                      [[16384, 128], [512, 4], [1, 128]])
                        scalar.copy(out=dst, in_=pT[:])
                        scalar.drain().then_inc(s_act, 1)
                    V["pT_guard"] = inc("act", 1)
                    emit("scalar", fa3)
                V["p0_prev_tr"] = cur("pe")
            V[f"xnT_done_{h}"] = cur("act")

        def tgu(h):
            xnT_done = V[f"xnT_done_{h}"]
            ld_misc = V["ld_misc"]
            for name, src, p0_, dstT in (("g", gas, 0, tgT), ("u", uas, 32, tuT)):
                prev_evac = V.get(f"t_evac_{name}", 0)
                for ic in range(NIC):
                    def fm(tensor, src=src, ic=ic, p0_=p0_, xnT_done=xnT_done,
                           ld_misc=ld_misc, prev_evac=prev_evac):
                        if ic == 0:
                            tensor.wait_ge(s_act, xnT_done)
                            tensor.wait_ge(s_lm, ld_misc)
                            if prev_evac:
                                tensor.wait_ge(s_act, prev_evac)
                        ins = tensor.matmul(pt3[p0_:p0_ + 16, :],
                                            bass.AP(src, ic * 16, [[NIC * 16, 128], [1, 16]]),
                                            bigv(ic),
                                            start=(ic == 0), stop=(ic == NIC - 1))
                        if ic == NIC - 1:
                            ins.then_inc(s_pe, 1)
                    emit("tensor", fm)
                pe_t = inc("pe", 1)
                def fe(scalar, pe_t=pe_t, dstT=dstT, p0_=p0_):
                    scalar.wait_ge(s_pe, pe_t)
                    scalar.copy(out=dstT[:], in_=pt3[p0_:p0_ + 16, :])
                    scalar.drain().then_inc(s_act, 1)
                V[f"t_evac_{name}"] = inc("act", 1)
                emit("scalar", fe)
            V[f"t_ready_{h}"] = cur("act")

        def dq_tile(idx, tile):
            mat, kc, ooff, ow = tile
            slot = idx % 2
            segs = [(so, min(512, ow - so)) for so in range(0, ow, 512)]
            nob = ow // 128

            prev_s15 = V.get(f"dq_s15_{idx - 2}", 0)
            prev_bc = V.get(f"dq_bc_{idx - 2}", 0)
            def fl(sync, mat=mat, kc=kc, ooff=ooff, ow=ow, slot=slot, prev_s15=prev_s15):
                if prev_s15:
                    sync.wait_ge(s_act, prev_s15)
                sync.dma_start_transpose(
                    out=bass.AP(cbuf, slot * 2048, [[2 * 2048, 128], [1, ow]]),
                    in_=cod_src[mat][ooff:ooff + ow, kc * 128:(kc + 1) * 128]).then_inc(s_ldc[slot], 16)
            ldc_t = inc(f"ldc{slot}", 16)
            emit("sync", fl)

            prev_pl = V.get(f"dq_pl_{idx - 2}", 0)
            def flam(scalar, mat=mat, kc=kc, ooff=ooff, ow=ow, slot=slot, prev_pl=prev_pl):
                if prev_pl:
                    scalar.wait_ge(s_pl, prev_pl)
                scalar.dma_start(out=bass.AP(amb, slot * 2048, [[2 * 2048, 128], [1, ow]]),
                                 in_=am_src[mat][kc, :, ooff:ooff + ow]).then_inc(s_lam[slot], 16)
            lam_t = inc(f"lam{slot}", 16)
            emit("scalar", flam)

            for v in range(1, 16):
                zs = (v - 1) % 3
                def fs(scalar, v=v, zs=zs, ow=ow, slot=slot, idx=idx, ldc_t=ldc_t):
                    if v == 1:
                        scalar.wait_ge(s_ldc[slot], ldc_t)
                    if v >= 4:
                        scalar.wait_ge(s_dve, V[f"dq_sv_{idx}_{v - 3}"])
                    elif idx >= 1:
                        scalar.wait_ge(s_dve, V[f"dq_sv_{idx - 1}_{12 + v}"])
                    ins = scalar.activation(out=bass.AP(zbuf, zs * 2048, [[3 * 2048, 128], [1, ow]]),
                                            in_=bass.AP(cbuf, slot * 2048, [[2 * 2048, 128], [1, ow]]),
                                            func=AF.Sign, bias=kconst[0:128, v:v + 1],
                                            scale=1.0)
                    if v == 15:
                        scalar.drain().then_inc(s_act, 1)
                    else:
                        ins.then_inc(s_act, 1)
                V[f"dq_sg_{idx}_{v}"] = inc("act", 1)
                emit("scalar", fs)
            V[f"dq_s15_{idx}"] = V[f"dq_sg_{idx}_15"]

            prev_store = V.get(f"dq_st_{idx - 2}", 0)
            for v in range(1, 16):
                zs = (v - 1) % 3
                def fv(vector, v=v, zs=zs, ow=ow, slot=slot, idx=idx, prev_store=prev_store):
                    vector.wait_ge(s_act, V[f"dq_sg_{idx}_{min(v + 1, 15)}"])
                    if v == 1 and prev_store:
                        vector.wait_ge(s_stw[slot], prev_store)
                    zin = bass.AP(zbuf, zs * 2048, [[3 * 2048, 128], [1, ow]])
                    wv = bass.AP(wacc, slot * 2048, [[2 * 2048, 128], [1, ow]])
                    if v == 1:
                        vector.scalar_tensor_tensor(out=wv, in0=zin, scalar=HD[0],
                                                    in1=bass.AP(kconst, 16, [[17, 128], [0, ow]]),
                                                    op0=AluOpType.mult,
                                                    op1=AluOpType.add).then_inc(s_dve, 1)
                    elif v < 15:
                        vector.scalar_tensor_tensor(out=wv, in0=zin, scalar=HD[v - 1],
                                                    in1=wv, op0=AluOpType.mult,
                                                    op1=AluOpType.add).then_inc(s_dve, 1)
                    else:
                        vector.scalar_tensor_tensor(out=wv, in0=zin, scalar=HD[v - 1],
                                                    in1=wv, op0=AluOpType.mult,
                                                    op1=AluOpType.add)
                        vector.drain().then_inc(s_dve, 1)
                V[f"dq_sv_{idx}_{v}"] = inc("dve", 1)
                emit("vector", fv)

            stt15 = V[f"dq_sv_{idx}_15"]
            def fpm(gpsimd, mat=mat, kc=kc, ooff=ooff, ow=ow, slot=slot, nob=nob,
                    stt15=stt15, lam_t=lam_t):
                gpsimd.wait_ge(s_dve, stt15)
                gpsimd.wait_ge(s_lam[slot], lam_t)
                wv = bass.AP(wacc, slot * 2048, [[2 * 2048, 128], [1, ow]])
                gpsimd.tensor_tensor(out=wv, in0=wv,
                                     in1=bass.AP(amb, slot * 2048, [[2 * 2048, 128], [1, ow]]),
                                     op=AluOpType.mult)
                gpsimd.drain().then_inc(s_pl, 1)
                if mat in ("g", "u"):
                    src = bass.AP(wacc, slot * 2048, [[2 * 2048, 128], [128, nob], [1, 128]])
                    dst = wt_dst[mat][ooff // 128: ooff // 128 + nob, :,
                                      kc * 128:(kc + 1) * 128].transpose([1, 0, 2])
                else:
                    src = bass.AP(wacc, slot * 2048, [[2 * 2048, 128], [1, ow]])
                    dst = wt_dst["d"][kc, :, ooff:ooff + ow]
                gpsimd.dma_start(out=dst, in_=src).then_inc(s_stw[slot], 16)
            V[f"dq_pl_{idx}"] = inc("pl", 1)
            V[f"dq_st_{idx}"] = inc(f"stw{slot}", 16)
            emit("gpsimd", fpm)

        def w_store_marker():
            c0, c1 = cur("stw0"), cur("stw1")
            def fmk(gpsimd, c0=c0, c1=c1):
                if c0:
                    gpsimd.wait_ge(s_stw[0], c0)
                if c1:
                    gpsimd.wait_ge(s_stw[1], c1)
                gpsimd.sem_inc(s_stb, 1)
            emit("gpsimd", fmk)
            return inc("stb", 1)

        def x3_store_marker():
            c0, c1 = cur("sx0"), cur("sx1")
            def fmk(gpsimd, c0=c0, c1=c1):
                if c0:
                    gpsimd.wait_ge(s_sx[0], c0)
                if c1:
                    gpsimd.wait_ge(s_sx[1], c1)
                gpsimd.sem_inc(s_stb, 1)
            emit("gpsimd", fmk)
            return inc("stb", 1)

        def p1_ob(h, ob):
            st_need = V.get(f"dq_oc_{ob // 16}", 0)
            if ob > 0:
                g_prev = V[f"p1_peg_{h}_{ob - 1}"]
                u_prev = V[f"p1_pe_{h}_{ob - 1}"]
            else:
                g_prev = u_prev = V.get(f"p1_pe_{1 - h}_{NFC - 1}", 0)
            lane_sem = {"g": s_ldg, "u": s_ldw}
            lane_cnt = {"g": "ldg", "u": "ldw"}
            lane_pe = {"g": g_prev, "u": u_prev}
            lane_lb = {"g": (gbs, gb_h), "u": (ubs, ub_h)}
            lane_tot = {}
            for mi, mat in enumerate(("g", "u")):
                sem = lane_sem[mat]
                pep = lane_pe[mat]
                lb_s, lb_h = lane_lb[mat]
                def flw(sync, mat=mat, mi=mi, ob=ob, st_need=st_need, pep=pep,
                        sem=sem, lb_s=lb_s, lb_h=lb_h):
                    if mi == 0:
                        sync.wait_ge(s_stb, st_need)
                    if pep:
                        sync.wait_ge(s_pe, pep)
                    sync.dma_start(out=bass.AP(wring, mi * 4096,
                                               [[2 * 4096, 128], [1, 2048]]),
                                   in_=wt_dst[mat][ob, :, 0:2048]).then_inc(sem, 16)
                    sync.dma_start(out=bass.AP(lb_s, (ob % 2) * 128, [[256, 16], [1, 128]]),
                                   in_=lb_h[:, ob * 128:(ob + 1) * 128]).then_inc(sem, 16)
                def flw2(scalar, mat=mat, mi=mi, ob=ob, st_need=st_need, pep=pep, sem=sem):
                    if mi == 0:
                        scalar.wait_ge(s_stb, st_need)
                    if pep:
                        scalar.wait_ge(s_pe, pep)
                    scalar.dma_start(out=bass.AP(wring, mi * 4096 + 2048,
                                                 [[2 * 4096, 128], [1, 2048]]),
                                     in_=wt_dst[mat][ob, :, 2048:4096]).then_inc(sem, 16)
                lane_tot[mat] = inc(lane_cnt[mat], 48)
                emit("sync", flw)
                emit("scalar", flw2)

            pya = pA if ob % 2 == 0 else pC
            pyb = pB if ob % 2 == 0 else pD
            t_ready = V[f"t_ready_{h}"]
            x3_guard = V.get(f"x3_dve_{h}_{ob - 2}", 0)
            ldg_t, ldw_t = lane_tot["g"], lane_tot["u"]
            def fmm(tensor, h=h, ob=ob, pya=pya, pyb=pyb, ldg_t=ldg_t, ldw_t=ldw_t,
                    t_ready=t_ready, x3_guard=x3_guard):
                if ob == 0:
                    tensor.wait_ge(s_act, t_ready)
                if x3_guard:
                    tensor.wait_ge(s_dve, x3_guard)
                for mi, (ppp, tT, lsl, sem, tot) in enumerate((
                        (pya, tgT, gbs, s_ldg, ldg_t), (pyb, tuT, ubs, s_ldw, ldw_t))):
                    tensor.wait_ge(sem, tot)
                    for ic in range(NIC):
                        tensor.matmul(ppp[:],
                                      bass.AP(wring, mi * 4096 + ic * 128,
                                              [[2 * 4096, 128], [1, 128]]),
                                      bigv(ic), start=(ic == 0), stop=False)
                    tensor.matmul(ppp[:],
                                  bass.AP(lsl, (ob % 2) * 128, [[256, 16], [1, 128]]),
                                  tT[:], start=False, stop=True).then_inc(s_pe, 1)
            pe_y = inc("pe", 2)
            V[f"p1_peg_{h}_{ob}"] = pe_y - 1
            V[f"p1_pe_{h}_{ob}"] = pe_y
            emit("tensor", fmm)

            def fre(scalar, pe_y=pe_y, ob=ob, pya=pya, x3_guard=x3_guard):
                scalar.wait_ge(s_pe, pe_y)
                if x3_guard:
                    scalar.wait_ge(s_dve, x3_guard)
                scalar.activation(out=bass.AP(x3t, (ob % 2) * 512, [[1024, 128], [1, 512]]),
                                  in_=pya[:], func=AF.Relu, bias=0.0, scale=1.0)
                scalar.drain().then_inc(s_act, 1)
            act_re = inc("act", 1)
            emit("scalar", fre)

            st_guard = V.get(f"x3_st_{h}_{ob - 2}", 0)
            if not st_guard and ob < 2:
                st_guard = V.get(f"x3_st_{1 - h}_{NFC - 2 + ob}", 0)
            def fx3(vector, act_re=act_re, ob=ob, pyb=pyb, st_guard=st_guard):
                vector.wait_ge(s_act, act_re)
                if st_guard:
                    vector.wait_ge(s_sx[ob % 2], st_guard)
                vector.tensor_tensor(out=bass.AP(x3t, (ob % 2) * 512, [[1024, 128], [1, 512]]),
                                     in0=bass.AP(x3t, (ob % 2) * 512, [[1024, 128], [1, 512]]),
                                     in1=pyb[:], op=AluOpType.mult)
                vector.drain().then_inc(s_dve, 1)
            dve_x3 = inc("dve", 1)
            V[f"x3_dve_{h}_{ob}"] = dve_x3
            emit("vector", fx3)

            td_guard = V.get("td_evac_0", 0) if (h == 1 and ob == 0) else 0
            def ftd(tensor, ob=ob, h=h, dve_x3=dve_x3, td_guard=td_guard):
                tensor.wait_ge(s_dve, dve_x3)
                if td_guard:
                    tensor.wait_ge(s_act, td_guard)
                tensor.matmul(pt3[64:80, :],
                              bass.AP(das, ob * 16, [[NFC * 16, 128], [1, 16]]),
                              bass.AP(x3t, (ob % 2) * 512, [[1024, 128], [1, 512]]),
                              start=(ob == 0), stop=(ob == NFC - 1)).then_inc(s_pe, 1)
            pe_td = inc("pe", 1)
            emit("tensor", ftd)
            if ob == NFC - 1:
                def fetd(scalar, pe_td=pe_td, h=h):
                    scalar.wait_ge(s_pe, pe_td)
                    scalar.copy(out=tdT[:, h * 512:(h + 1) * 512],
                                in_=pt3[64:80, :])
                    scalar.drain().then_inc(s_act, 1)
                V[f"td_evac_{h}"] = inc("act", 1)
                emit("scalar", fetd)

            def fsx(gpsimd, ob=ob, h=h, dve_x3=dve_x3):
                gpsimd.wait_ge(s_dve, dve_x3)
                gpsimd.dma_start(out=x3_scr[ob, :, h * 512:(h + 1) * 512],
                                 in_=bass.AP(x3t, (ob % 2) * 512, [[1024, 128], [1, 512]])).then_inc(s_sx[ob % 2], 16)
            V[f"x3_st_{h}_{ob}"] = inc(f"sx{ob % 2}", 16)
            emit("gpsimd", fsx)

        P2_BANKS = [pA, pB, pC, pD, pT, pt3, pamA, pamB]
        P2_LANES = [("sync", s_ldw, "ldw"), ("gpsimd", s_ldg, "ldg"),
                    ("sync", s_lda[0], "lda0"), ("gpsimd", s_lda[1], "lda1")]
        P2_PREV_LANE_FC = {0: 84, 1: 85, 2: 82, 3: 83}

        def p2(h):
            pe_before = cur("pe")
            out_prev = V.get("out_store", 0)
            r_all = cur("sr")
            def fri(sync, h=h, pe_before=pe_before, out_prev=out_prev, r_all=r_all):
                sync.wait_ge(s_pe, pe_before)
                if out_prev:
                    sync.wait_ge(s_so, out_prev)
                sync.wait_ge(s_sr, r_all)
                sync.dma_start(out=bass.AP(big, 0, [[16384, 128], [4096, 4], [1, 4096]]),
                               in_=r_scr.rearrange("(hh sb p) d -> hh p sb d", hh=2, p=128)[h]
                               ).then_inc(s_lr, 16)
            ldr = inc("lr", 16)
            emit("sync", fri)

            x3_all = V[f"x3_done_{h}"]
            for ps in range(4):
                blk_i = ps // 2
                prev_last = V.get("p2_last_mm", 0)
                prev_evac = V.get("p2_last_evac", 0)
                td_ev = V.get("td_evac_1", 0)
                x3g = V.get("x3_dve_1_85", 0)

                db_gate = prev_last if prev_last else pe_before
                def fldb(sync, ps=ps, db_gate=db_gate):
                    sync.wait_ge(s_pe, db_gate)
                    for j in range(2):
                        sync.dma_start(out=bass.AP(dbs, j * 512, [[1024, 16], [1, 512]]),
                                       in_=db_h[:, ps * 1024 + j * 512: ps * 1024 + (j + 1) * 512]
                                       ).then_inc(s_lm, 16)
                db_t = inc("lm", 32)
                emit("sync", fldb)

                for g in range(NG):
                    fcs = list(range(g * 8, min(g * 8 + 8, NFC)))
                    buf = xta if g % 2 == 0 else xtb
                    gp = g % 2
                    grp_guard = V.get(f"p2_buf_{gp}", 0) or (0 if gp == 0 else pe_before)
                    def flx(scalar, fcs=fcs, buf=buf, h=h, gp=gp, grp_guard=grp_guard,
                            x3_all=x3_all, r_all=r_all):
                        if grp_guard:
                            scalar.wait_ge(s_pe, grp_guard)
                        if gp == 0:
                            scalar.wait_ge(s_sr, r_all)
                        scalar.wait_ge(s_stb, x3_all)
                        for k, fc in enumerate(fcs):
                            scalar.dma_start(out=bass.AP(buf, k * 512, [[4096, 128], [1, 512]]),
                                             in_=x3_scr[fc, :, h * 512:(h + 1) * 512]).then_inc(s_lg[gp], 16)
                    inc(f"lg{gp}", 16 * len(fcs))
                    emit("scalar", flx)
                    ldx_grp = cur(f"lg{gp}")

                    for k, fc in enumerate(fcs):
                        lane = fc % 4
                        lq, lsem, lcnt = P2_LANES[lane]
                        if fc >= 4:
                            gate = V[f"p2_mm_{h}_{ps}_{fc - 4}"]
                        elif ps > 0:
                            gate = V[f"p2_mm_{h}_{ps - 1}_{P2_PREV_LANE_FC[lane]}"]
                        elif h == 1:
                            gate = V[f"p2_mm_{0}_{3}_{P2_PREV_LANE_FC[lane]}"]
                        else:
                            gate = pe_before
                        dmk = V.get(f"dq_d_{blk_i}_{fc // 8}", 0)
                        def flwd(eng, fc=fc, lane=lane, ps=ps, gate=gate, dmk=dmk, lsem=lsem):
                            if gate:
                                eng.wait_ge(s_pe, gate)
                            if dmk:
                                eng.wait_ge(s_stb, dmk)
                            eng.dma_start(out=bass.AP(wring, lane * 1024,
                                                      [[2 * 4096, 128], [1, 1024]]),
                                          in_=wt_dst["d"][fc, :, ps * 1024:(ps + 1) * 1024]
                                          ).then_inc(lsem, 16)
                        ld_t = inc(lcnt, 16)
                        emit(lq, flwd)

                        def fm2(tensor, fc=fc, k=k, g=g, gp=gp, lane=lane, buf=buf, h=h,
                                ldx_grp=ldx_grp, ld_t=ld_t, lsem=lsem, db_t=db_t,
                                td_ev=td_ev, prev_evac=prev_evac, x3g=x3g):
                            if k == 0:
                                tensor.wait_ge(s_lg[gp], ldx_grp)
                            if fc == 0:
                                if prev_evac:
                                    tensor.wait_ge(s_dve, prev_evac)
                                else:
                                    if x3g:
                                        tensor.wait_ge(s_dve, x3g)
                                    if td_ev:
                                        tensor.wait_ge(s_act, td_ev)
                            tensor.wait_ge(lsem, ld_t)
                            ins = None
                            for sb in range(4):
                                for j in range(2):
                                    b = P2_BANKS[sb * 2 + j]
                                    ins = tensor.matmul(b[:],
                                                        bass.AP(buf, k * 512 + sb * 128,
                                                                [[4096, 128], [1, 128]]),
                                                        bass.AP(wring, lane * 1024 + j * 512,
                                                                [[2 * 4096, 128], [1, 512]]),
                                                        start=(fc == 0), stop=False)
                            if fc == NFC - 1:
                                tensor.wait_ge(s_lm, db_t)
                                if td_ev:
                                    tensor.wait_ge(s_act, td_ev)
                                for sb in range(4):
                                    for j in range(2):
                                        b = P2_BANKS[sb * 2 + j]
                                        ins = tensor.matmul(b[:],
                                                            bass.AP(tdT, h * 512 + sb * 128,
                                                                    [[1024, 16], [1, 128]]),
                                                            bass.AP(dbs, j * 512, [[1024, 16], [1, 512]]),
                                                            start=False, stop=True)
                            ins.then_inc(s_pe, 1)
                        V[f"p2_mm_{h}_{ps}_{fc}"] = inc("pe", 1)
                        emit("tensor", fm2)
                    V[f"p2_buf_{gp}"] = cur("pe")

                last_mm = V[f"p2_mm_{h}_{ps}_{NFC - 1}"]
                V["p2_last_mm"] = last_mm
                for sb in range(4):
                    for j in range(2):
                        b = P2_BANKS[sb * 2 + j]
                        def fev(vector, sb=sb, j=j, ps=ps, b=b, last_mm=last_mm, ldr=ldr):
                            vector.wait_ge(s_pe, last_mm)
                            vector.wait_ge(s_lr, ldr)
                            dst = accv(sb, ps * 1024 + j * 512, 512)
                            vector.tensor_tensor(out=dst, in0=dst, in1=b[:],
                                                 op=AluOpType.add)
                            vector.drain().then_inc(s_dve, 1)
                        V[f"p2_ev_{h}_{ps}_{sb}_{j}"] = inc("dve", 1)
                        emit("vector", fev)
                V["p2_last_evac"] = cur("dve")

                for sb in range(4):
                    ev = V[f"p2_ev_{h}_{ps}_{sb}_1"]
                    def fso(gpsimd, sb=sb, ps=ps, h=h, ev=ev):
                        gpsimd.wait_ge(s_dve, ev)
                        gpsimd.dma_start(out=out_h[h * 512 + sb * 128: h * 512 + (sb + 1) * 128,
                                                   ps * 1024:(ps + 1) * 1024],
                                         in_=accv(sb, ps * 1024, 1024)).then_inc(s_so, 16)
                    inc("so", 16)
                    emit("gpsimd", fso)
            V["out_store"] = cur("so")

        seq = [0]
        di = [0]

        def emit_tile(tile):
            dq_tile(seq[0], tile)
            seq[0] += 1

        def emit_d(target):
            while di[0] < min(target, len(D_TILES)):
                tile = D_TILES[di[0]]
                emit_tile(tile)
                dfc = di[0] % NFC
                if dfc % 8 == 7 or dfc == NFC - 1:
                    V[f"dq_d_{di[0] // NFC}_{dfc // 8}"] = w_store_marker()
                di[0] += 1

        p0(0)
        if phases == "p0":
            p0(1)
        if phases in ("p1h0", "full"):
            tgu(0)
            ob_done = 0
            pending = []
            for oci, (oo, ow_) in enumerate(G_OCS):
                ti, k = 0, 0
                while ti < 2 * NIC or k < len(pending):
                    for _ in range(4):
                        if ti < 2 * NIC:
                            emit_tile(GU_TILES[oci * 2 * NIC + ti])
                            ti += 1
                    if oci == len(G_OCS) - 1 and phases == "full":
                        emit_d(di[0] + 2)
                    if k < len(pending):
                        p1_ob(0, pending[k])
                        k += 1
                V[f"dq_oc_{oci}"] = w_store_marker()
                nob_end = (oo + ow_) // 128
                pending = list(range(ob_done, nob_end))
                ob_done = nob_end
            for ob in pending:
                if phases == "full":
                    emit_d(di[0] + 3)
                p1_ob(0, ob)
            V["x3_done_0"] = x3_store_marker()

        if phases == "full":
            emit_d(di[0] + 16)
            w_store_marker()
            p0(1)
            tgu(1)
            for ob in range(NFC):
                emit_d(di[0] + 2)
                p1_ob(1, ob)
            emit_d(len(D_TILES))
            V["x3_done_1"] = x3_store_marker()

            p2(0)
            p2(1)

        if debug:
            dve_now = cur("dve")
            ndump = 1 if phases.startswith("p0") else 5
            def fdbg(gpsimd, dve_now=dve_now, ndump=ndump):
                gpsimd.wait_ge(s_dve, dve_now)
                gpsimd.dma_start(out=dbg_r[:], in_=r_scr[:]).then_inc(s_st, 16)
                if ndump > 1:
                    for fc in range(4):
                        gpsimd.dma_start(out=dbg_x3[fc], in_=x3_scr[fc]).then_inc(s_st, 16)
                    wsel = [("g", 0, 0), ("g", 0, 31), ("g", 50, 0), ("g", 85, 31),
                            ("u", 0, 0), ("u", 85, 31), ("d", 0, 0), ("d", 85, 31)]
                    for wi, (wm, wa, wb) in enumerate(wsel):
                        gpsimd.dma_start(out=dbg_w[wi],
                                         in_=wt_dst[wm][wa, :, wb * 128:(wb + 1) * 128]).then_inc(s_st, 16)
            inc("st", 16 * (ndump + 8 if ndump > 1 else ndump))
            emit("gpsimd", fdbg)

        so_final = cur("so")
        st_final = cur("st")
        def ffin(gpsimd, so_final=so_final, st_final=st_final):
            if so_final:
                gpsimd.wait_ge(s_so, so_final)
            if st_final:
                gpsimd.wait_ge(s_st, st_final)
        emit("gpsimd", ffin)

        @blk.sync
        def _(sync):
            for f in ops["sync"]:
                f(sync)

        @blk.tensor
        def _(tensor):
            for f in ops["tensor"]:
                f(tensor)

        @blk.vector
        def _(vector):
            for f in ops["vector"]:
                f(vector)

        @blk.scalar
        def _(scalar):
            for f in ops["scalar"]:
                f(scalar)

        @blk.gpsimd
        def _(gpsimd):
            for f in ops["gpsimd"]:
                f(gpsimd)

    return nc


def kernel(L0_attn_x, L0_ffn_g_codes, L0_ffn_g_absmax, L0_ffn_g_lora_a, L0_ffn_g_lora_b,
           L0_ffn_u_codes, L0_ffn_u_absmax, L0_ffn_u_lora_a, L0_ffn_u_lora_b,
           L0_ffn_d_codes, L0_ffn_d_absmax, L0_ffn_d_lora_a, L0_ffn_d_lora_b):
    x = np.ascontiguousarray(np.asarray(L0_attn_x, dtype=np.float32).reshape(8192, 4096))

    def lo16(a):
        a = np.ascontiguousarray(np.asarray(a, dtype="<i4"))
        return np.ascontiguousarray(a.view("<u2")[:, 0::2])

    def _amx(am, nkc):
        amT = np.asarray(am, dtype=np.float32).T
        return np.ascontiguousarray(np.repeat(amT, 64, axis=0).reshape(nkc, 128, amT.shape[1]))

    common = {
        "g16": lo16(L0_ffn_g_codes), "u16": lo16(L0_ffn_u_codes), "d16": lo16(L0_ffn_d_codes),
        "amxg": _amx(L0_ffn_g_absmax, NIC),
        "amxu": _amx(L0_ffn_u_absmax, NIC),
        "amxd": _amx(L0_ffn_d_absmax, NFC),
        "ga": np.ascontiguousarray(np.asarray(L0_ffn_g_lora_a, dtype=np.float32)),
        "gb": np.ascontiguousarray(np.asarray(L0_ffn_g_lora_b, dtype=np.float32)),
        "ua": np.ascontiguousarray(np.asarray(L0_ffn_u_lora_a, dtype=np.float32)),
        "ub": np.ascontiguousarray(np.asarray(L0_ffn_u_lora_b, dtype=np.float32)),
        "da": np.ascontiguousarray(np.asarray(L0_ffn_d_lora_a, dtype=np.float32)),
        "db": np.ascontiguousarray(np.asarray(L0_ffn_d_lora_b, dtype=np.float32)),
        "ident": np.eye(128, dtype=np.float32),
    }
    nc = build_nc()
    in_maps = [dict(common, x=np.ascontiguousarray(x[k * 1024:(k + 1) * 1024]))
               for k in range(8)]
    res = run_bass_kernel_spmd(nc, in_maps, core_ids=list(range(8)))
    out = np.concatenate([np.asarray(r["out"]) for r in res.results], axis=0)
    return out.reshape(1, 8192, 4096)


# version 09
# speedup vs baseline: 1.3571x; 1.3571x over previous
import numpy as np
import concourse.bass as bass
import concourse.mybir as mybir
from concourse.bass_utils import run_bass_kernel_spmd
from concourse.alu_op_type import AluOpType

F32 = mybir.dt.float32
U16 = mybir.dt.uint16
BF16 = mybir.dt.bfloat16
AF = mybir.ActivationFunctionType
AX = mybir.AxisListType

NF4 = np.array([-1.0, -0.6961928009986877, -0.5250730514526367, -0.39491748809814453,
                -0.28444138169288635, -0.18477343022823334, -0.09105003625154495, 0.0,
                0.07958029955625534, 0.16093020141124725, 0.2461123019456863,
                0.33791524171829224, 0.44070982933044434, 0.5626170039176941,
                0.7229568362236023, 1.0], dtype=np.float32)
DV = (NF4[1:] - NF4[:-1]).astype(np.float32)
HD = [float(np.float32(DV[v - 1]) * np.float32(0.5)) for v in range(1, 16)]

D = 4096
FF = 11008
SC = 1024
H = 512
NIC = 32
NFC = 86
NOT = 8
NG = 11
EPS = 1e-5

G_OCS = [(0, 2048), (2048, 2048), (4096, 2048), (6144, 2048), (8192, 2048), (10240, 768)]
D_OCS = [(0, 2048), (2048, 2048)]
GU_TILES = []
for _oci, (_oo, _ow) in enumerate(G_OCS):
    for _kc in range(NIC):
        GU_TILES.append(("g", _kc, _oo, _ow))
        GU_TILES.append(("u", _kc, _oo, _ow))
D_TILES = []
for (_oo, _ow) in D_OCS:
    for _fc in range(NFC):
        D_TILES.append(("d", _fc, _oo, _ow))
DQ_TILES = GU_TILES + D_TILES


def build_nc(debug=False, phases="full"):
    nc = bass.Bass("TRN2", target_bir_lowering=False, detect_race_conditions=False)
    dbg_r = dbg_x3 = None
    if debug:
        dbg_r = nc.dram_tensor("dbg_r", [SC, D], F32, kind="ExternalOutput").ap()
        dbg_x3 = nc.dram_tensor("dbg_x3", [4, 128, SC], F32, kind="ExternalOutput").ap()
        dbg_w = nc.dram_tensor("dbg_w", [8, 128, 128], F32, kind="ExternalOutput").ap()

    x_h = nc.dram_tensor("x", [SC, D], F32, kind="ExternalInput").ap()
    cod_src = {
        "g": nc.dram_tensor("g16", [FF, D], U16, kind="ExternalInput").ap(),
        "u": nc.dram_tensor("u16", [FF, D], U16, kind="ExternalInput").ap(),
        "d": nc.dram_tensor("d16", [D, FF], U16, kind="ExternalInput").ap(),
    }
    am_src = {
        "g": nc.dram_tensor("amxg", [NIC, 128, FF], F32, kind="ExternalInput").ap(),
        "u": nc.dram_tensor("amxu", [NIC, 128, FF], F32, kind="ExternalInput").ap(),
        "d": nc.dram_tensor("amxd", [NFC, 128, D], F32, kind="ExternalInput").ap(),
    }
    ga_h = nc.dram_tensor("ga", [D, 16], F32, kind="ExternalInput").ap()
    gb_h = nc.dram_tensor("gb", [16, FF], F32, kind="ExternalInput").ap()
    ua_h = nc.dram_tensor("ua", [D, 16], F32, kind="ExternalInput").ap()
    ub_h = nc.dram_tensor("ub", [16, FF], F32, kind="ExternalInput").ap()
    da_h = nc.dram_tensor("da", [FF, 16], F32, kind="ExternalInput").ap()
    db_h = nc.dram_tensor("db", [16, D], F32, kind="ExternalInput").ap()
    id_h = nc.dram_tensor("ident", [128, 128], F32, kind="ExternalInput").ap()
    out_h = nc.dram_tensor("out", [SC, D], F32, kind="ExternalOutput").ap()

    r_scr = nc.dram_tensor("r_scr", [SC, D], F32, kind="Internal").ap()
    x3_scr = nc.dram_tensor("x3_scr", [NFC, 128, SC], F32, kind="Internal").ap()
    wt_dst = {
        "g": nc.dram_tensor("wtg", [NFC, NIC, 128, 128], F32, kind="Internal").ap(),
        "u": nc.dram_tensor("wtu", [NFC, NIC, 128, 128], F32, kind="Internal").ap(),
        "d": nc.dram_tensor("wtd", [NFC, NIC, 128, 128], F32, kind="Internal").ap(),
    }

    V = {}
    CNT = {}

    def inc(sem_name, by):
        CNT[sem_name] = CNT.get(sem_name, 0) + by
        return CNT[sem_name]

    def cur(sem_name):
        return CNT.get(sem_name, 0)

    ops = {"sync": [], "tensor": [], "vector": [], "scalar": [], "gpsimd": []}

    def emit(eng, fn):
        ops[eng].append(fn)

    from contextlib import ExitStack
    with ExitStack() as ctx:
        e = ctx.enter_context
        big = e(nc.sbuf_tensor("big", [128, 16384], F32))
        xta = e(nc.sbuf_tensor("xta", [128, D], F32))
        xtb = e(nc.sbuf_tensor("xtb", [128, D], F32))
        cbuf = e(nc.sbuf_tensor("cbuf", [128, 2 * 2048], U16))
        zbuf = e(nc.sbuf_tensor("zbuf", [128, 3 * 2048], BF16))
        wacc = e(nc.sbuf_tensor("wacc", [128, 2 * 2048], F32))
        amb = e(nc.sbuf_tensor("amb", [128, 2 * 2048], F32))
        wring = e(nc.sbuf_tensor("wring", [128, 2 * 4096], F32))
        x3t = e(nc.sbuf_tensor("x3t", [128, 2 * 512], F32))
        gas = e(nc.sbuf_tensor("gas", [128, NIC * 16], F32))
        uas = e(nc.sbuf_tensor("uas", [128, NIC * 16], F32))
        das = e(nc.sbuf_tensor("das", [128, NFC * 16], F32))
        gbs = e(nc.sbuf_tensor("gbs", [16, 2 * 128], F32))
        ubs = e(nc.sbuf_tensor("ubs", [16, 2 * 128], F32))
        dbs = e(nc.sbuf_tensor("dbs", [16, 2 * 512], F32))
        tgT = bass.AP(dbs, 0, [[1024, 16], [1, 512]])
        tuT = bass.AP(dbs, 512, [[1024, 16], [1, 512]])
        tdT = e(nc.sbuf_tensor("tdT", [16, 1024], F32))
        ident = e(nc.sbuf_tensor("ident_sb", [128, 128], F32))
        st1 = e(nc.sbuf_tensor("st1", [128, 1], F32))
        st2 = e(nc.sbuf_tensor("st2", [128, 1], F32))
        stm = st1[:]
        stv = e(nc.sbuf_tensor("stv", [128, 1], F32))
        strs = e(nc.sbuf_tensor("strs", [128, 1], F32))
        stn = e(nc.sbuf_tensor("stn", [128, 1], F32))
        kconst = e(nc.sbuf_tensor("kconst", [128, 17], F32))
        pA = e(nc.psum_tensor("pA", [128, 512], F32))
        pB = e(nc.psum_tensor("pB", [128, 512], F32))
        pC = e(nc.psum_tensor("pC", [128, 512], F32))
        pD = e(nc.psum_tensor("pD", [128, 512], F32))
        pT = e(nc.psum_tensor("pT", [128, 512], F32))
        pt3 = e(nc.psum_tensor("pt3", [128, 512], F32))
        pamA = e(nc.psum_tensor("pamA", [128, 512], F32))
        pamB = e(nc.psum_tensor("pamB", [128, 512], F32))
        s_lm = e(nc.semaphore("s_lm"))
        s_lx = e(nc.semaphore("s_lx"))
        s_ldc = [e(nc.semaphore("s_ldc0")), e(nc.semaphore("s_ldc1"))]
        s_lda = [e(nc.semaphore("s_lda0")), e(nc.semaphore("s_lda1"))]
        s_ldw = e(nc.semaphore("s_ldw"))
        s_ldg = e(nc.semaphore("s_ldg"))
        s_lg = [e(nc.semaphore("s_lg0")), e(nc.semaphore("s_lg1"))]
        s_lr = e(nc.semaphore("s_lr"))
        s_sr = e(nc.semaphore("s_sr"))
        s_sx = [e(nc.semaphore("s_sx0")), e(nc.semaphore("s_sx1"))]
        s_stw = [e(nc.semaphore("s_stw0")), e(nc.semaphore("s_stw1"))]
        s_so = e(nc.semaphore("s_so"))
        s_stb = e(nc.semaphore("s_stb"))
        s_pl = e(nc.semaphore("s_pl"))
        s_lam = [e(nc.semaphore("s_lam0")), e(nc.semaphore("s_lam1"))]
        s_act = e(nc.semaphore("s_act"))
        s_dve = e(nc.semaphore("s_dve"))
        s_pe = e(nc.semaphore("s_pe"))
        s_st = e(nc.semaphore("s_st"))
        blk = e(nc.Block())
        def bigv(ic, w=512):
            return bass.AP(big, ic * 512, [[16384, 128], [1, w]])

        def accv(sb, off, w):
            return bass.AP(big, sb * 4096 + off, [[16384, 128], [1, w]])

        def lora_loads():
            def f(sync):
                sync.dma_start(out=ident[:], in_=id_h[:]).then_inc(s_lm, 16)
                sync.dma_start(out=bass.AP(gas, 0, [[NIC * 16, 128], [16, NIC], [1, 16]]),
                               in_=ga_h.rearrange("(c p) r -> p c r", p=128)).then_inc(s_lm, 16)
                sync.dma_start(out=bass.AP(uas, 0, [[NIC * 16, 128], [16, NIC], [1, 16]]),
                               in_=ua_h.rearrange("(c p) r -> p c r", p=128)).then_inc(s_lm, 16)
                sync.dma_start(out=bass.AP(das, 0, [[NFC * 16, 128], [16, NFC], [1, 16]]),
                               in_=da_h.rearrange("(c p) r -> p c r", p=128)).then_inc(s_lm, 16)
            emit("sync", f)
            V["ld_misc"] = inc("lm", 64)

        lora_loads()

        def const_init():
            def f(vector):
                vector.memset(kconst[:, 0:1], EPS)
                for v in range(1, 15):
                    vector.memset(kconst[:, v:v + 1], float(-(v - 0.5)))
                vector.memset(kconst[:, 15:16], float(-(15 - 0.5)))
                vector.memset(kconst[:, 16:17], 0.0)
                vector.drain().then_inc(s_dve, 1)
            emit("vector", f)
            V["consts"] = inc("dve", 1)
        const_init()

        def p0(h):
            pe_guard = cur("pe")
            first_evac_done = [False]
            for st in range(4):
                row0 = h * H + st * 128
                prev_r_store = V.get("last_r_store", 0)
                def fl(sync, row0=row0, prev_r_store=prev_r_store):
                    if prev_r_store:
                        sync.wait_ge(s_sr, prev_r_store)
                    sync.dma_start(out=xta[:], in_=x_h[row0:row0 + 128, :]).then_inc(s_lx, 16)
                ldx_t = inc("lx", 16)
                emit("sync", fl)

                prev_xn = V.get("p0_prev_xn", 0)
                wacc_free = cur("stb")
                def fv(vector, ldx_t=ldx_t, prev_xn=prev_xn, wacc_free=wacc_free):
                    vector.wait_ge(s_lx, ldx_t)
                    if prev_xn:
                        vector.wait_ge(s_act, prev_xn)
                    if wacc_free:
                        vector.wait_ge(s_stb, wacc_free)
                    zz = bass.AP(wacc, 0, [[2 * 2048, 128], [1, D]])
                    vector.reduce_sum(st1[:], xta[:], axis=AX.X)
                    vector.tensor_tensor(out=zz, in0=xta[:], in1=xta[:], op=AluOpType.mult)
                    vector.reduce_sum(st2[:], zz, axis=AX.X)
                    vector.drain()
                    vector.tensor_scalar(out=stm, in0=st1[:], scalar1=1.0 / D,
                                         scalar2=None, op0=AluOpType.mult)
                    vector.tensor_scalar(out=st2[:], in0=st2[:], scalar1=1.0 / D,
                                         scalar2=None, op0=AluOpType.mult)
                    vector.drain()
                    vector.tensor_tensor(out=stv[:], in0=stm, in1=stm, op=AluOpType.mult)
                    vector.drain()
                    vector.tensor_tensor(out=stv[:], in0=st2[:], in1=stv[:],
                                         op=AluOpType.subtract)
                    vector.drain().then_inc(s_dve, 1)
                dve_stats = inc("dve", 1)
                emit("vector", fv)

                def fa(scalar, dve_stats=dve_stats):
                    scalar.wait_ge(s_dve, dve_stats)
                    scalar.wait_ge(s_dve, V["consts"])
                    scalar.activation(out=strs[:], in_=stv[:], func=AF.Sqrt,
                                      bias=kconst[0:128, 0:1], scale=1.0)
                    scalar.drain().then_inc(s_act, 1)
                act_rstd = inc("act", 1)
                emit("scalar", fa)

                def fv2(vector, act_rstd=act_rstd):
                    vector.wait_ge(s_act, act_rstd)
                    vector.reciprocal(out=strs[:], in_=strs[:])
                    vector.drain()
                    vector.tensor_scalar(out=stn[:], in0=stm, scalar1=strs[0:128, 0:1],
                                         scalar2=-1.0, op0=AluOpType.mult,
                                         op1=AluOpType.mult)
                    vector.drain().then_inc(s_dve, 1)
                dve_stn = inc("dve", 1)
                emit("vector", fv2)

                prev_tr = V.get("p0_prev_tr", 0)
                def fa2(scalar, dve_stn=dve_stn, prev_tr=prev_tr):
                    scalar.wait_ge(s_dve, dve_stn)
                    if prev_tr:
                        scalar.wait_ge(s_pe, prev_tr)
                    scalar.activation(out=xtb[:], in_=xta[:], func=AF.Identity,
                                      bias=stn[0:128, 0:1], scale=strs[0:128, 0:1])
                    scalar.drain().then_inc(s_act, 1)
                act_xn = inc("act", 1)
                V["p0_prev_xn"] = act_xn
                emit("scalar", fa2)

                def fv3(vector, act_xn=act_xn):
                    vector.wait_ge(s_act, act_xn)
                    vector.scalar_tensor_tensor(out=xta[:], in0=xtb[:], scalar=2.0,
                                                in1=xta[:], op0=AluOpType.mult,
                                                op1=AluOpType.add)
                    vector.drain().then_inc(s_dve, 1)
                dve_r = inc("dve", 1)
                emit("vector", fv3)

                def fp(gpsimd, row0=row0, dve_r=dve_r):
                    gpsimd.wait_ge(s_dve, dve_r)
                    gpsimd.dma_start(out=r_scr[row0:row0 + 128, :], in_=xta[:]).then_inc(s_sr, 16)
                V["last_r_store"] = inc("sr", 16)
                emit("gpsimd", fp)
                V[f"r_store_{h}"] = V["last_r_store"]

                for icb in range(8):
                    pT_guard = V.get("pT_guard", 0)
                    ld_misc = V["ld_misc"]
                    def ft(tensor, icb=icb, act_xn=act_xn, pT_guard=pT_guard, ld_misc=ld_misc):
                        tensor.wait_ge(s_act, act_xn)
                        tensor.wait_ge(s_lm, ld_misc)
                        if pT_guard:
                            tensor.wait_ge(s_act, pT_guard)
                        for j in range(4):
                            ic = icb * 4 + j
                            tensor.transpose(bass.AP(pT, j * 128, [[512, 128], [1, 128]]),
                                             xtb[:, ic * 128:(ic + 1) * 128],
                                             ident[:]).then_inc(s_pe, 1)
                    pe_tr = inc("pe", 4)
                    emit("tensor", ft)

                    need_pe_guard = pe_guard if not first_evac_done[0] else 0
                    first_evac_done[0] = True
                    def fa3(scalar, icb=icb, st=st, pe_tr=pe_tr, need_pe_guard=need_pe_guard):
                        scalar.wait_ge(s_pe, pe_tr)
                        if need_pe_guard:
                            scalar.wait_ge(s_pe, need_pe_guard)
                        dst = bass.AP(big, (icb * 4) * 512 + st * 128,
                                      [[16384, 128], [512, 4], [1, 128]])
                        scalar.copy(out=dst, in_=pT[:])
                        scalar.drain().then_inc(s_act, 1)
                    V["pT_guard"] = inc("act", 1)
                    emit("scalar", fa3)
                V["p0_prev_tr"] = cur("pe")
            V[f"xnT_done_{h}"] = cur("act")

        def tgu(h):
            xnT_done = V[f"xnT_done_{h}"]
            ld_misc = V["ld_misc"]
            for name, src, p0_, dstT in (("g", gas, 0, tgT), ("u", uas, 32, tuT)):
                prev_evac = V.get(f"t_evac_{name}", 0)
                for ic in range(NIC):
                    def fm(tensor, src=src, ic=ic, p0_=p0_, xnT_done=xnT_done,
                           ld_misc=ld_misc, prev_evac=prev_evac):
                        if ic == 0:
                            tensor.wait_ge(s_act, xnT_done)
                            tensor.wait_ge(s_lm, ld_misc)
                            if prev_evac:
                                tensor.wait_ge(s_act, prev_evac)
                        ins = tensor.matmul(pt3[p0_:p0_ + 16, :],
                                            bass.AP(src, ic * 16, [[NIC * 16, 128], [1, 16]]),
                                            bigv(ic),
                                            start=(ic == 0), stop=(ic == NIC - 1))
                        if ic == NIC - 1:
                            ins.then_inc(s_pe, 1)
                    emit("tensor", fm)
                pe_t = inc("pe", 1)
                def fe(scalar, pe_t=pe_t, dstT=dstT, p0_=p0_):
                    scalar.wait_ge(s_pe, pe_t)
                    scalar.copy(out=dstT[:], in_=pt3[p0_:p0_ + 16, :])
                    scalar.drain().then_inc(s_act, 1)
                V[f"t_evac_{name}"] = inc("act", 1)
                emit("scalar", fe)
            V[f"t_ready_{h}"] = cur("act")

        def dq_tile(idx):
            mat, kc, ooff, ow = DQ_TILES[idx]
            slot = idx % 2
            segs = [(so, min(512, ow - so)) for so in range(0, ow, 512)]
            nob = ow // 128

            prev_s15 = V.get(f"dq_s15_{idx - 2}", 0)
            prev_bc = V.get(f"dq_bc_{idx - 2}", 0)
            def fl(sync, mat=mat, kc=kc, ooff=ooff, ow=ow, slot=slot, prev_s15=prev_s15):
                if prev_s15:
                    sync.wait_ge(s_act, prev_s15)
                sync.dma_start_transpose(
                    out=bass.AP(cbuf, slot * 2048, [[2 * 2048, 128], [1, ow]]),
                    in_=cod_src[mat][ooff:ooff + ow, kc * 128:(kc + 1) * 128]).then_inc(s_ldc[slot], 16)
            ldc_t = inc(f"ldc{slot}", 16)
            emit("sync", fl)

            prev_pl = V.get(f"dq_pl_{idx - 2}", 0)
            def flam(scalar, mat=mat, kc=kc, ooff=ooff, ow=ow, slot=slot, prev_pl=prev_pl):
                if prev_pl:
                    scalar.wait_ge(s_pl, prev_pl)
                scalar.dma_start(out=bass.AP(amb, slot * 2048, [[2 * 2048, 128], [1, ow]]),
                                 in_=am_src[mat][kc, :, ooff:ooff + ow]).then_inc(s_lam[slot], 16)
            lam_t = inc(f"lam{slot}", 16)
            emit("scalar", flam)

            for v in range(1, 16):
                zs = (v - 1) % 3
                def fs(scalar, v=v, zs=zs, ow=ow, slot=slot, idx=idx, ldc_t=ldc_t):
                    if v == 1:
                        scalar.wait_ge(s_ldc[slot], ldc_t)
                    if v >= 4:
                        scalar.wait_ge(s_dve, V[f"dq_sv_{idx}_{v - 3}"])
                    elif idx >= 1:
                        scalar.wait_ge(s_dve, V[f"dq_sv_{idx - 1}_{12 + v}"])
                    ins = scalar.activation(out=bass.AP(zbuf, zs * 2048, [[3 * 2048, 128], [1, ow]]),
                                            in_=bass.AP(cbuf, slot * 2048, [[2 * 2048, 128], [1, ow]]),
                                            func=AF.Sign, bias=kconst[0:128, v:v + 1],
                                            scale=1.0)
                    if v == 15:
                        scalar.drain().then_inc(s_act, 1)
                    else:
                        ins.then_inc(s_act, 1)
                V[f"dq_sg_{idx}_{v}"] = inc("act", 1)
                emit("scalar", fs)
            V[f"dq_s15_{idx}"] = V[f"dq_sg_{idx}_15"]

            prev_store = V.get(f"dq_st_{idx - 2}", 0)
            for v in range(1, 16):
                zs = (v - 1) % 3
                def fv(vector, v=v, zs=zs, ow=ow, slot=slot, idx=idx, prev_store=prev_store):
                    vector.wait_ge(s_act, V[f"dq_sg_{idx}_{min(v + 1, 15)}"])
                    if v == 1 and prev_store:
                        vector.wait_ge(s_stw[slot], prev_store)
                    zin = bass.AP(zbuf, zs * 2048, [[3 * 2048, 128], [1, ow]])
                    wv = bass.AP(wacc, slot * 2048, [[2 * 2048, 128], [1, ow]])
                    if v == 1:
                        vector.scalar_tensor_tensor(out=wv, in0=zin, scalar=HD[0],
                                                    in1=bass.AP(kconst, 16, [[17, 128], [0, ow]]),
                                                    op0=AluOpType.mult,
                                                    op1=AluOpType.add).then_inc(s_dve, 1)
                    elif v < 15:
                        vector.scalar_tensor_tensor(out=wv, in0=zin, scalar=HD[v - 1],
                                                    in1=wv, op0=AluOpType.mult,
                                                    op1=AluOpType.add).then_inc(s_dve, 1)
                    else:
                        vector.scalar_tensor_tensor(out=wv, in0=zin, scalar=HD[v - 1],
                                                    in1=wv, op0=AluOpType.mult,
                                                    op1=AluOpType.add)
                        vector.drain().then_inc(s_dve, 1)
                V[f"dq_sv_{idx}_{v}"] = inc("dve", 1)
                emit("vector", fv)

            stt15 = V[f"dq_sv_{idx}_15"]
            def fpm(gpsimd, mat=mat, kc=kc, ooff=ooff, ow=ow, slot=slot, nob=nob,
                    stt15=stt15, lam_t=lam_t):
                gpsimd.wait_ge(s_dve, stt15)
                gpsimd.wait_ge(s_lam[slot], lam_t)
                wv = bass.AP(wacc, slot * 2048, [[2 * 2048, 128], [1, ow]])
                gpsimd.tensor_tensor(out=wv, in0=wv,
                                     in1=bass.AP(amb, slot * 2048, [[2 * 2048, 128], [1, ow]]),
                                     op=AluOpType.mult)
                gpsimd.drain().then_inc(s_pl, 1)
                src = bass.AP(wacc, slot * 2048, [[2 * 2048, 128], [128, nob], [1, 128]])
                if mat in ("g", "u"):
                    dst = wt_dst[mat][ooff // 128: ooff // 128 + nob, kc]
                else:
                    dst = wt_dst["d"][kc, ooff // 128: ooff // 128 + nob]
                gpsimd.dma_start(out=dst.transpose([1, 0, 2]), in_=src).then_inc(s_stw[slot], 16)
            V[f"dq_pl_{idx}"] = inc("pl", 1)
            V[f"dq_st_{idx}"] = inc(f"stw{slot}", 16)
            emit("gpsimd", fpm)

        def w_store_marker():
            c0, c1 = cur("stw0"), cur("stw1")
            def fmk(gpsimd, c0=c0, c1=c1):
                if c0:
                    gpsimd.wait_ge(s_stw[0], c0)
                if c1:
                    gpsimd.wait_ge(s_stw[1], c1)
                gpsimd.sem_inc(s_stb, 1)
            emit("gpsimd", fmk)
            return inc("stb", 1)

        def x3_store_marker():
            c0, c1 = cur("sx0"), cur("sx1")
            def fmk(gpsimd, c0=c0, c1=c1):
                if c0:
                    gpsimd.wait_ge(s_sx[0], c0)
                if c1:
                    gpsimd.wait_ge(s_sx[1], c1)
                gpsimd.sem_inc(s_stb, 1)
            emit("gpsimd", fmk)
            return inc("stb", 1)

        def p1_ob(h, ob):
            st_need = V.get(f"dq_oc_{ob // 16}", 0)
            if ob > 0:
                g_prev = V[f"p1_peg_{h}_{ob - 1}"]
                u_prev = V[f"p1_pe_{h}_{ob - 1}"]
            else:
                g_prev = u_prev = V.get(f"p1_pe_{1 - h}_{NFC - 1}", 0)
            lane_sem = {"g": s_ldg, "u": s_ldw}
            lane_cnt = {"g": "ldg", "u": "ldw"}
            lane_pe = {"g": g_prev, "u": u_prev}
            lane_lb = {"g": (gbs, gb_h), "u": (ubs, ub_h)}
            lane_tot = {}
            for mi, mat in enumerate(("g", "u")):
                sem = lane_sem[mat]
                pep = lane_pe[mat]
                lb_s, lb_h = lane_lb[mat]
                def flw(sync, mat=mat, mi=mi, ob=ob, st_need=st_need, pep=pep,
                        sem=sem, lb_s=lb_s, lb_h=lb_h):
                    if mi == 0:
                        sync.wait_ge(s_stb, st_need)
                    if pep:
                        sync.wait_ge(s_pe, pep)
                    for grp in range(2):
                        sync.dma_start(out=bass.AP(wring, mi * 4096 + grp * 2048,
                                                   [[2 * 4096, 128], [128, 16], [1, 128]]),
                                       in_=wt_dst[mat][ob, grp * 16:(grp + 1) * 16]
                                       .transpose([1, 0, 2])).then_inc(sem, 16)
                    sync.dma_start(out=bass.AP(lb_s, (ob % 2) * 128, [[256, 16], [1, 128]]),
                                   in_=lb_h[:, ob * 128:(ob + 1) * 128]).then_inc(sem, 16)
                lane_tot[mat] = inc(lane_cnt[mat], 48)
                emit("sync", flw)

            pya = (pA, pC, pamA)[ob % 3]
            pyb = (pB, pD, pamB)[ob % 3]
            t_ready = V[f"t_ready_{h}"]
            x3_guard = V.get(f"x3_dve_{h}_{ob - 2}", 0)
            pe_x3g = V.get(f"x3_dve_{h}_{ob - 3}", 0)
            ldg_t, ldw_t = lane_tot["g"], lane_tot["u"]
            def fmm(tensor, h=h, ob=ob, pya=pya, pyb=pyb, ldg_t=ldg_t, ldw_t=ldw_t,
                    t_ready=t_ready, pe_x3g=pe_x3g):
                if ob == 0:
                    tensor.wait_ge(s_act, t_ready)
                if pe_x3g:
                    tensor.wait_ge(s_dve, pe_x3g)
                for mi, (ppp, tT, lsl, sem, tot) in enumerate((
                        (pya, tgT, gbs, s_ldg, ldg_t), (pyb, tuT, ubs, s_ldw, ldw_t))):
                    tensor.wait_ge(sem, tot)
                    for ic in range(NIC):
                        tensor.matmul(ppp[:],
                                      bass.AP(wring, mi * 4096 + ic * 128,
                                              [[2 * 4096, 128], [1, 128]]),
                                      bigv(ic), start=(ic == 0), stop=False)
                    tensor.matmul(ppp[:],
                                  bass.AP(lsl, (ob % 2) * 128, [[256, 16], [1, 128]]),
                                  tT[:], start=False, stop=True).then_inc(s_pe, 1)
            pe_y = inc("pe", 2)
            V[f"p1_peg_{h}_{ob}"] = pe_y - 1
            V[f"p1_pe_{h}_{ob}"] = pe_y
            emit("tensor", fmm)

            st_guard = V.get(f"x3_st_{h}_{ob - 2}", 0)
            if not st_guard and ob < 2:
                st_guard = V.get(f"x3_st_{1 - h}_{NFC - 2 + ob}", 0)

            def fre(scalar, pe_y=pe_y, ob=ob, pya=pya, x3_guard=x3_guard, st_guard=st_guard):
                scalar.wait_ge(s_pe, pe_y)
                if x3_guard:
                    scalar.wait_ge(s_dve, x3_guard)
                if st_guard:
                    scalar.wait_ge(s_sx[ob % 2], st_guard)
                scalar.activation(out=bass.AP(x3t, (ob % 2) * 512, [[1024, 128], [1, 512]]),
                                  in_=pya[:], func=AF.Relu, bias=0.0, scale=1.0)
                scalar.drain().then_inc(s_act, 1)
            act_re = inc("act", 1)
            emit("scalar", fre)
            def fx3(vector, act_re=act_re, ob=ob, pyb=pyb, st_guard=st_guard):
                vector.wait_ge(s_act, act_re)
                if st_guard:
                    vector.wait_ge(s_sx[ob % 2], st_guard)
                vector.tensor_tensor(out=bass.AP(x3t, (ob % 2) * 512, [[1024, 128], [1, 512]]),
                                     in0=bass.AP(x3t, (ob % 2) * 512, [[1024, 128], [1, 512]]),
                                     in1=pyb[:], op=AluOpType.mult)
                vector.drain().then_inc(s_dve, 1)
            dve_x3 = inc("dve", 1)
            V[f"x3_dve_{h}_{ob}"] = dve_x3
            emit("vector", fx3)

            td_guard = V.get("td_evac_0", 0) if (h == 1 and ob == 0) else 0
            def ftd(tensor, ob=ob, h=h, dve_x3=dve_x3, td_guard=td_guard):
                tensor.wait_ge(s_dve, dve_x3)
                if td_guard:
                    tensor.wait_ge(s_act, td_guard)
                tensor.matmul(pt3[64:80, :],
                              bass.AP(das, ob * 16, [[NFC * 16, 128], [1, 16]]),
                              bass.AP(x3t, (ob % 2) * 512, [[1024, 128], [1, 512]]),
                              start=(ob == 0), stop=(ob == NFC - 1)).then_inc(s_pe, 1)
            pe_td = inc("pe", 1)
            emit("tensor", ftd)
            if ob == NFC - 1:
                def fetd(scalar, pe_td=pe_td, h=h):
                    scalar.wait_ge(s_pe, pe_td)
                    scalar.copy(out=tdT[:, h * 512:(h + 1) * 512],
                                in_=pt3[64:80, :])
                    scalar.drain().then_inc(s_act, 1)
                V[f"td_evac_{h}"] = inc("act", 1)
                emit("scalar", fetd)

            def fsx(gpsimd, ob=ob, h=h, dve_x3=dve_x3):
                gpsimd.wait_ge(s_dve, dve_x3)
                gpsimd.dma_start(out=x3_scr[ob, :, h * 512:(h + 1) * 512],
                                 in_=bass.AP(x3t, (ob % 2) * 512, [[1024, 128], [1, 512]])).then_inc(s_sx[ob % 2], 16)
            V[f"x3_st_{h}_{ob}"] = inc(f"sx{ob % 2}", 16)
            emit("gpsimd", fsx)

        def p2(h):
            pe_before = cur("pe")
            out_prev = V.get("out_store", 0)
            r_done = V[f"r_store_{h}"]
            r_all = cur("sr")
            def fri(sync, h=h, pe_before=pe_before, out_prev=out_prev, r_all=r_all):
                sync.wait_ge(s_pe, pe_before)
                if out_prev:
                    sync.wait_ge(s_so, out_prev)
                sync.wait_ge(s_sr, r_all)
                sync.dma_start(out=bass.AP(big, 0, [[16384, 128], [4096, 4], [1, 4096]]),
                               in_=r_scr.rearrange("(hh sb p) d -> hh p sb d", hh=2, p=128)[h]
                               ).then_inc(s_lr, 16)
            ldr = inc("lr", 16)
            emit("sync", fri)

            x3_all = V[f"x3_done_{h}"]
            bank_guard = {0: 0, 1: 0}
            wd_guard = {0: V.get("p2_wd_free", 0), 1: V.get("p2_wd_free", 0)}
            for g in range(NG):
                fcs = list(range(g * 8, min(g * 8 + 8, NFC)))
                buf = xta if g % 2 == 0 else xtb
                grp_guard = V.get(f"p2_pe_{h}_{g - 2}", 0) or (pe_before if g < 2 else 0)
                gp = g % 2
                def flx(scalar, fcs=fcs, buf=buf, h=h, gp=gp, grp_guard=grp_guard, x3_all=x3_all):
                    if grp_guard:
                        scalar.wait_ge(s_pe, grp_guard)
                    scalar.wait_ge(s_stb, x3_all)
                    for k, fc in enumerate(fcs):
                        scalar.dma_start(out=bass.AP(buf, k * 512, [[4096, 128], [1, 512]]),
                                         in_=x3_scr[fc, :, h * 512:(h + 1) * 512]).then_inc(s_lg[gp], 16)
                inc(f"lg{gp}", 16 * len(fcs))
                emit("scalar", flx)
                ldx_grp = cur(f"lg{gp}")

                for ot in range(NOT):
                    otp = ot % 2
                    wq = "sync" if otp == 0 else "gpsimd"
                    wsem = s_ldw if otp == 0 else s_ldg
                    wcnt = "ldw" if otp == 0 else "ldg"
                    wg = wd_guard[otp]
                    dmark = V.get("dq_d_mark", 0)
                    last_g = g == NG - 1
                    def flwd(eng, fcs=fcs, ot=ot, otp=otp, h=h, wg=wg, dmark=dmark,
                             wsem=wsem, last_g=last_g):
                        if wg:
                            eng.wait_ge(s_pe, wg)
                        if dmark:
                            eng.wait_ge(s_stb, dmark)
                        for k, fc in enumerate(fcs):
                            eng.dma_start(out=bass.AP(wring, otp * 4096 + k * 512,
                                                      [[2 * 4096, 128], [128, 4], [1, 128]]),
                                          in_=wt_dst["d"][fc, ot * 4:(ot + 1) * 4]
                                          .transpose([1, 0, 2])).then_inc(wsem, 16)
                        if last_g:
                            eng.dma_start(out=bass.AP(dbs, otp * 512, [[1024, 16], [1, 512]]),
                                          in_=db_h[:, ot * 512:(ot + 1) * 512]).then_inc(wsem, 16)
                    inc(wcnt, 16 * (len(fcs) + (1 if last_g else 0)))
                    emit(wq, flwd)
                    ldw_last = cur(wcnt)

                    for sb in range(4):
                        par = (ot * 4 + sb) % 2
                        pp = pA if par == 0 else pB
                        bg = bank_guard[par]
                        td_ev = V.get("td_evac_1", 0)
                        def fm2(tensor, g=g, ot=ot, otp=otp, sb=sb, fcs=fcs, buf=buf, pp=pp, h=h,
                                ldx_grp=ldx_grp, ldw_last=ldw_last, bg=bg, td_ev=td_ev,
                                wsem=wsem):
                            if sb == 0:
                                tensor.wait_ge(s_lg[g % 2], ldx_grp)
                                tensor.wait_ge(wsem, ldw_last)
                            if bg:
                                tensor.wait_ge(s_dve, bg)
                            last_g = g == NG - 1
                            for k in range(len(fcs)):
                                ins = tensor.matmul(pp[:],
                                                    bass.AP(buf, k * 512 + sb * 128,
                                                            [[4096, 128], [1, 128]]),
                                                    bass.AP(wring, otp * 4096 + k * 512,
                                                            [[2 * 4096, 128], [1, 512]]),
                                                    start=(k == 0),
                                                    stop=(not last_g and k == len(fcs) - 1))
                            if last_g:
                                tensor.wait_ge(s_act, td_ev)
                                ins = tensor.matmul(pp[:],
                                                    bass.AP(tdT, h * 512 + sb * 128, [[1024, 16], [1, 128]]),
                                                    bass.AP(dbs, otp * 512, [[1024, 16], [1, 512]]),
                                                    start=False, stop=True)
                            ins.then_inc(s_pe, 1)
                        pe_mm2 = inc("pe", 1)
                        emit("tensor", fm2)

                        def fad(vector, sb=sb, ot=ot, pp=pp, pe_mm2=pe_mm2, g=g, ldr=ldr):
                            vector.wait_ge(s_pe, pe_mm2)
                            if g == 0 and ot == 0 and sb < 2:
                                vector.wait_ge(s_lr, ldr)
                            dst = accv(sb, ot * 512, 512)
                            vector.tensor_tensor(out=dst, in0=dst, in1=pp[:],
                                                 op=AluOpType.add)
                            vector.drain().then_inc(s_dve, 1)
                        bank_guard[par] = inc("dve", 1)
                        emit("vector", fad)
                    wd_guard[otp] = cur("pe")
                V[f"p2_pe_{h}_{g}"] = cur("pe")
            V["p2_wd_free"] = cur("pe")
            dve_fin = cur("dve")

            for sb in range(4):
                def fso(gpsimd, sb=sb, h=h, dve_fin=dve_fin):
                    gpsimd.wait_ge(s_dve, dve_fin)
                    gpsimd.dma_start(out=out_h[h * 512 + sb * 128: h * 512 + (sb + 1) * 128, :],
                                     in_=accv(sb, 0, 4096)).then_inc(s_so, 16)
                inc("so", 16)
                emit("gpsimd", fso)
            V["out_store"] = cur("so")

        p0(0)
        if phases == "p0":
            p0(1)
        if phases in ("p1h0", "full"):
            tgu(0)
            ob_done = 0
            pending = []
            for oci, (oo, ow_) in enumerate(G_OCS):
                base = oci * 2 * NIC
                ti, k = 0, 0
                while ti < 2 * NIC or k < len(pending):
                    for _ in range(4):
                        if ti < 2 * NIC:
                            dq_tile(base + ti)
                            ti += 1
                    if k < len(pending):
                        p1_ob(0, pending[k])
                        k += 1
                V[f"dq_oc_{oci}"] = w_store_marker()
                nob_end = (oo + ow_) // 128
                pending = list(range(ob_done, nob_end))
                ob_done = nob_end
            for ob in pending:
                p1_ob(0, ob)
            V["x3_done_0"] = x3_store_marker()

        if phases == "full":
            p0(1)
            tgu(1)
            di = 0
            for ob in range(NFC):
                for _ in range(2):
                    if di < len(D_TILES):
                        dq_tile(len(GU_TILES) + di)
                        di += 1
                p1_ob(1, ob)
            while di < len(D_TILES):
                dq_tile(len(GU_TILES) + di)
                di += 1
            V["dq_d_mark"] = w_store_marker()
            V["x3_done_1"] = x3_store_marker()

            p2(0)
            p2(1)

        if debug:
            dve_now = cur("dve")
            ndump = 1 if phases.startswith("p0") else 5
            def fdbg(gpsimd, dve_now=dve_now, ndump=ndump):
                gpsimd.wait_ge(s_dve, dve_now)
                gpsimd.dma_start(out=dbg_r[:], in_=r_scr[:]).then_inc(s_st, 16)
                if ndump > 1:
                    for fc in range(4):
                        gpsimd.dma_start(out=dbg_x3[fc], in_=x3_scr[fc]).then_inc(s_st, 16)
                    wsel = [("g", 0, 0), ("g", 0, 31), ("g", 50, 0), ("g", 85, 31),
                            ("u", 0, 0), ("u", 85, 31), ("d", 0, 0), ("d", 85, 31)]
                    for wi, (wm, wa, wb) in enumerate(wsel):
                        gpsimd.dma_start(out=dbg_w[wi], in_=wt_dst[wm][wa, wb]).then_inc(s_st, 16)
            inc("st", 16 * (ndump + 8 if ndump > 1 else ndump))
            emit("gpsimd", fdbg)

        so_final = cur("so")
        st_final = cur("st")
        def ffin(gpsimd, so_final=so_final, st_final=st_final):
            if so_final:
                gpsimd.wait_ge(s_so, so_final)
            if st_final:
                gpsimd.wait_ge(s_st, st_final)
        emit("gpsimd", ffin)

        @blk.sync
        def _(sync):
            for f in ops["sync"]:
                f(sync)

        @blk.tensor
        def _(tensor):
            for f in ops["tensor"]:
                f(tensor)

        @blk.vector
        def _(vector):
            for f in ops["vector"]:
                f(vector)

        @blk.scalar
        def _(scalar):
            for f in ops["scalar"]:
                f(scalar)

        @blk.gpsimd
        def _(gpsimd):
            for f in ops["gpsimd"]:
                f(gpsimd)

    return nc


def kernel(L0_attn_x, L0_ffn_g_codes, L0_ffn_g_absmax, L0_ffn_g_lora_a, L0_ffn_g_lora_b,
           L0_ffn_u_codes, L0_ffn_u_absmax, L0_ffn_u_lora_a, L0_ffn_u_lora_b,
           L0_ffn_d_codes, L0_ffn_d_absmax, L0_ffn_d_lora_a, L0_ffn_d_lora_b):
    x = np.ascontiguousarray(np.asarray(L0_attn_x, dtype=np.float32).reshape(8192, 4096))

    def lo16(a):
        a = np.ascontiguousarray(np.asarray(a, dtype="<i4"))
        return np.ascontiguousarray(a.view("<u2")[:, 0::2])

    def _amx(am, nkc):
        amT = np.asarray(am, dtype=np.float32).T
        return np.ascontiguousarray(np.repeat(amT, 64, axis=0).reshape(nkc, 128, amT.shape[1]))

    common = {
        "g16": lo16(L0_ffn_g_codes), "u16": lo16(L0_ffn_u_codes), "d16": lo16(L0_ffn_d_codes),
        "amxg": _amx(L0_ffn_g_absmax, NIC),
        "amxu": _amx(L0_ffn_u_absmax, NIC),
        "amxd": _amx(L0_ffn_d_absmax, NFC),
        "ga": np.ascontiguousarray(np.asarray(L0_ffn_g_lora_a, dtype=np.float32)),
        "gb": np.ascontiguousarray(np.asarray(L0_ffn_g_lora_b, dtype=np.float32)),
        "ua": np.ascontiguousarray(np.asarray(L0_ffn_u_lora_a, dtype=np.float32)),
        "ub": np.ascontiguousarray(np.asarray(L0_ffn_u_lora_b, dtype=np.float32)),
        "da": np.ascontiguousarray(np.asarray(L0_ffn_d_lora_a, dtype=np.float32)),
        "db": np.ascontiguousarray(np.asarray(L0_ffn_d_lora_b, dtype=np.float32)),
        "ident": np.eye(128, dtype=np.float32),
    }
    nc = build_nc()
    in_maps = [dict(common, x=np.ascontiguousarray(x[k * 1024:(k + 1) * 1024]))
               for k in range(8)]
    res = run_bass_kernel_spmd(nc, in_maps, core_ids=list(range(8)))
    out = np.concatenate([np.asarray(r["out"]) for r in res.results], axis=0)
    return out.reshape(1, 8192, 4096)


# version 10 (final)
# speedup vs baseline: 1.4251x; 1.0501x over previous
import numpy as np
import concourse.bass as bass
import concourse.mybir as mybir
from concourse.bass_utils import run_bass_kernel_spmd
from concourse.alu_op_type import AluOpType

F32 = mybir.dt.float32
U16 = mybir.dt.uint16
BF16 = mybir.dt.bfloat16
AF = mybir.ActivationFunctionType
AX = mybir.AxisListType

NF4 = np.array([-1.0, -0.6961928009986877, -0.5250730514526367, -0.39491748809814453,
                -0.28444138169288635, -0.18477343022823334, -0.09105003625154495, 0.0,
                0.07958029955625534, 0.16093020141124725, 0.2461123019456863,
                0.33791524171829224, 0.44070982933044434, 0.5626170039176941,
                0.7229568362236023, 1.0], dtype=np.float32)
DV = (NF4[1:] - NF4[:-1]).astype(np.float32)
HD = [float(np.float32(DV[v - 1]) * np.float32(0.5)) for v in range(1, 16)]

D = 4096
FF = 11008
SC = 1024
H = 512
NIC = 32
NFC = 86
NOT = 8
NG = 11
EPS = 1e-5

G_OCS = [(0, 2048), (2048, 2048), (4096, 2048), (6144, 2048), (8192, 2048), (10240, 768)]
D_OCS = [(0, 2048), (2048, 2048)]
GU_TILES = []
for _oci, (_oo, _ow) in enumerate(G_OCS):
    for _kc in range(NIC):
        GU_TILES.append(("g", _kc, _oo, _ow))
        GU_TILES.append(("u", _kc, _oo, _ow))
D_TILES = []
for (_oo, _ow) in D_OCS:
    for _fc in range(NFC):
        D_TILES.append(("d", _fc, _oo, _ow))
DQ_TILES = GU_TILES + D_TILES


def build_nc(debug=False, phases="full"):
    nc = bass.Bass("TRN2", target_bir_lowering=False, detect_race_conditions=False)
    dbg_r = dbg_x3 = None
    if debug:
        dbg_r = nc.dram_tensor("dbg_r", [SC, D], F32, kind="ExternalOutput").ap()
        dbg_x3 = nc.dram_tensor("dbg_x3", [4, 128, SC], F32, kind="ExternalOutput").ap()
        dbg_w = nc.dram_tensor("dbg_w", [8, 128, 128], F32, kind="ExternalOutput").ap()

    x_h = nc.dram_tensor("x", [SC, D], F32, kind="ExternalInput").ap()
    cod_src = {
        "g": nc.dram_tensor("g16", [FF, D], U16, kind="ExternalInput").ap(),
        "u": nc.dram_tensor("u16", [FF, D], U16, kind="ExternalInput").ap(),
        "d": nc.dram_tensor("d16", [D, FF], U16, kind="ExternalInput").ap(),
    }
    am_src = {
        "g": nc.dram_tensor("amxg", [NIC, 128, FF], F32, kind="ExternalInput").ap(),
        "u": nc.dram_tensor("amxu", [NIC, 128, FF], F32, kind="ExternalInput").ap(),
        "d": nc.dram_tensor("amxd", [NFC, 128, D], F32, kind="ExternalInput").ap(),
    }
    ga_h = nc.dram_tensor("ga", [D, 16], F32, kind="ExternalInput").ap()
    gb_h = nc.dram_tensor("gb", [16, FF], F32, kind="ExternalInput").ap()
    ua_h = nc.dram_tensor("ua", [D, 16], F32, kind="ExternalInput").ap()
    ub_h = nc.dram_tensor("ub", [16, FF], F32, kind="ExternalInput").ap()
    da_h = nc.dram_tensor("da", [FF, 16], F32, kind="ExternalInput").ap()
    db_h = nc.dram_tensor("db", [16, D], F32, kind="ExternalInput").ap()
    id_h = nc.dram_tensor("ident", [128, 128], F32, kind="ExternalInput").ap()
    out_h = nc.dram_tensor("out", [SC, D], F32, kind="ExternalOutput").ap()

    r_scr = nc.dram_tensor("r_scr", [SC, D], F32, kind="Internal").ap()
    x3_scr = nc.dram_tensor("x3_scr", [NFC, 128, SC], F32, kind="Internal").ap()
    wt_dst = {
        "g": nc.dram_tensor("wtg", [NFC, NIC, 128, 128], F32, kind="Internal").ap(),
        "u": nc.dram_tensor("wtu", [NFC, NIC, 128, 128], F32, kind="Internal").ap(),
        "d": nc.dram_tensor("wtd", [NFC, NIC, 128, 128], F32, kind="Internal").ap(),
    }

    V = {}
    CNT = {}

    def inc(sem_name, by):
        CNT[sem_name] = CNT.get(sem_name, 0) + by
        return CNT[sem_name]

    def cur(sem_name):
        return CNT.get(sem_name, 0)

    ops = {"sync": [], "tensor": [], "vector": [], "scalar": [], "gpsimd": []}

    def emit(eng, fn):
        ops[eng].append(fn)

    from contextlib import ExitStack
    with ExitStack() as ctx:
        e = ctx.enter_context
        big = e(nc.sbuf_tensor("big", [128, 16384], F32))
        xta = e(nc.sbuf_tensor("xta", [128, D], F32))
        xtb = e(nc.sbuf_tensor("xtb", [128, D], F32))
        cbuf = e(nc.sbuf_tensor("cbuf", [128, 2 * 2048], U16))
        zbuf = e(nc.sbuf_tensor("zbuf", [128, 3 * 2048], BF16))
        wacc = e(nc.sbuf_tensor("wacc", [128, 2 * 2048], F32))
        amb = e(nc.sbuf_tensor("amb", [128, 2 * 2048], F32))
        wring = e(nc.sbuf_tensor("wring", [128, 2 * 4096], F32))
        x3t = e(nc.sbuf_tensor("x3t", [128, 2 * 512], F32))
        gas = e(nc.sbuf_tensor("gas", [128, NIC * 16], F32))
        uas = e(nc.sbuf_tensor("uas", [128, NIC * 16], F32))
        das = e(nc.sbuf_tensor("das", [128, NFC * 16], F32))
        gbs = e(nc.sbuf_tensor("gbs", [16, 2 * 128], F32))
        ubs = e(nc.sbuf_tensor("ubs", [16, 2 * 128], F32))
        dbs = e(nc.sbuf_tensor("dbs", [16, 2 * 512], F32))
        tgT = bass.AP(dbs, 0, [[1024, 16], [1, 512]])
        tuT = bass.AP(dbs, 512, [[1024, 16], [1, 512]])
        tdT = e(nc.sbuf_tensor("tdT", [16, 1024], F32))
        ident = e(nc.sbuf_tensor("ident_sb", [128, 128], F32))
        st1 = e(nc.sbuf_tensor("st1", [128, 1], F32))
        st2 = e(nc.sbuf_tensor("st2", [128, 1], F32))
        stm = st1[:]
        stv = e(nc.sbuf_tensor("stv", [128, 1], F32))
        strs = e(nc.sbuf_tensor("strs", [128, 1], F32))
        stn = e(nc.sbuf_tensor("stn", [128, 1], F32))
        kconst = e(nc.sbuf_tensor("kconst", [128, 17], F32))
        pA = e(nc.psum_tensor("pA", [128, 512], F32))
        pB = e(nc.psum_tensor("pB", [128, 512], F32))
        pC = e(nc.psum_tensor("pC", [128, 512], F32))
        pD = e(nc.psum_tensor("pD", [128, 512], F32))
        pT = e(nc.psum_tensor("pT", [128, 512], F32))
        pt3 = e(nc.psum_tensor("pt3", [128, 512], F32))
        pamA = e(nc.psum_tensor("pamA", [128, 512], F32))
        pamB = e(nc.psum_tensor("pamB", [128, 512], F32))
        s_lm = e(nc.semaphore("s_lm"))
        s_lx = e(nc.semaphore("s_lx"))
        s_ldc = [e(nc.semaphore("s_ldc0")), e(nc.semaphore("s_ldc1"))]
        s_lda = [e(nc.semaphore("s_lda0")), e(nc.semaphore("s_lda1"))]
        s_ldw = e(nc.semaphore("s_ldw"))
        s_ldg = e(nc.semaphore("s_ldg"))
        s_lg = [e(nc.semaphore("s_lg0")), e(nc.semaphore("s_lg1"))]
        s_lr = e(nc.semaphore("s_lr"))
        s_sr = e(nc.semaphore("s_sr"))
        s_sx = [e(nc.semaphore("s_sx0")), e(nc.semaphore("s_sx1"))]
        s_stw = [e(nc.semaphore("s_stw0")), e(nc.semaphore("s_stw1"))]
        s_so = e(nc.semaphore("s_so"))
        s_stb = e(nc.semaphore("s_stb"))
        s_pl = e(nc.semaphore("s_pl"))
        s_lam = [e(nc.semaphore("s_lam0")), e(nc.semaphore("s_lam1"))]
        s_act = e(nc.semaphore("s_act"))
        s_dve = e(nc.semaphore("s_dve"))
        s_pe = e(nc.semaphore("s_pe"))
        s_st = e(nc.semaphore("s_st"))
        blk = e(nc.Block())
        def bigv(ic, w=512):
            return bass.AP(big, ic * 512, [[16384, 128], [1, w]])

        def accv(sb, off, w):
            return bass.AP(big, sb * 4096 + off, [[16384, 128], [1, w]])

        def lora_loads():
            def f(sync):
                sync.dma_start(out=ident[:], in_=id_h[:]).then_inc(s_lm, 16)
                sync.dma_start(out=bass.AP(gas, 0, [[NIC * 16, 128], [16, NIC], [1, 16]]),
                               in_=ga_h.rearrange("(c p) r -> p c r", p=128)).then_inc(s_lm, 16)
                sync.dma_start(out=bass.AP(uas, 0, [[NIC * 16, 128], [16, NIC], [1, 16]]),
                               in_=ua_h.rearrange("(c p) r -> p c r", p=128)).then_inc(s_lm, 16)
                sync.dma_start(out=bass.AP(das, 0, [[NFC * 16, 128], [16, NFC], [1, 16]]),
                               in_=da_h.rearrange("(c p) r -> p c r", p=128)).then_inc(s_lm, 16)
            emit("sync", f)
            V["ld_misc"] = inc("lm", 64)

        lora_loads()

        def const_init():
            def f(vector):
                vector.memset(kconst[:, 0:1], EPS)
                for v in range(1, 15):
                    vector.memset(kconst[:, v:v + 1], float(-(v - 0.5)))
                vector.memset(kconst[:, 15:16], float(-(15 - 0.5)))
                vector.memset(kconst[:, 16:17], 0.0)
                vector.drain().then_inc(s_dve, 1)
            emit("vector", f)
            V["consts"] = inc("dve", 1)
        const_init()

        def p0(h):
            pe_guard = cur("pe")
            first_evac_done = [False]
            for st in range(4):
                row0 = h * H + st * 128
                prev_r_store = V.get("last_r_store", 0)
                def fl(sync, row0=row0, prev_r_store=prev_r_store):
                    if prev_r_store:
                        sync.wait_ge(s_sr, prev_r_store)
                    sync.dma_start(out=xta[:], in_=x_h[row0:row0 + 128, :]).then_inc(s_lx, 16)
                ldx_t = inc("lx", 16)
                emit("sync", fl)

                prev_xn = V.get("p0_prev_xn", 0)
                wacc_free = cur("stb")
                def fv(vector, ldx_t=ldx_t, prev_xn=prev_xn, wacc_free=wacc_free):
                    vector.wait_ge(s_lx, ldx_t)
                    if prev_xn:
                        vector.wait_ge(s_act, prev_xn)
                    if wacc_free:
                        vector.wait_ge(s_stb, wacc_free)
                    zz = bass.AP(wacc, 0, [[2 * 2048, 128], [1, D]])
                    vector.reduce_sum(st1[:], xta[:], axis=AX.X)
                    vector.tensor_tensor(out=zz, in0=xta[:], in1=xta[:], op=AluOpType.mult)
                    vector.reduce_sum(st2[:], zz, axis=AX.X)
                    vector.drain()
                    vector.tensor_scalar(out=stm, in0=st1[:], scalar1=1.0 / D,
                                         scalar2=None, op0=AluOpType.mult)
                    vector.tensor_scalar(out=st2[:], in0=st2[:], scalar1=1.0 / D,
                                         scalar2=None, op0=AluOpType.mult)
                    vector.drain()
                    vector.tensor_tensor(out=stv[:], in0=stm, in1=stm, op=AluOpType.mult)
                    vector.drain()
                    vector.tensor_tensor(out=stv[:], in0=st2[:], in1=stv[:],
                                         op=AluOpType.subtract)
                    vector.drain().then_inc(s_dve, 1)
                dve_stats = inc("dve", 1)
                emit("vector", fv)

                def fa(scalar, dve_stats=dve_stats):
                    scalar.wait_ge(s_dve, dve_stats)
                    scalar.wait_ge(s_dve, V["consts"])
                    scalar.activation(out=strs[:], in_=stv[:], func=AF.Sqrt,
                                      bias=kconst[0:128, 0:1], scale=1.0)
                    scalar.drain().then_inc(s_act, 1)
                act_rstd = inc("act", 1)
                emit("scalar", fa)

                def fv2(vector, act_rstd=act_rstd):
                    vector.wait_ge(s_act, act_rstd)
                    vector.reciprocal(out=strs[:], in_=strs[:])
                    vector.drain()
                    vector.tensor_scalar(out=stn[:], in0=stm, scalar1=strs[0:128, 0:1],
                                         scalar2=-1.0, op0=AluOpType.mult,
                                         op1=AluOpType.mult)
                    vector.drain().then_inc(s_dve, 1)
                dve_stn = inc("dve", 1)
                emit("vector", fv2)

                prev_tr = V.get("p0_prev_tr", 0)
                def fa2(scalar, dve_stn=dve_stn, prev_tr=prev_tr):
                    scalar.wait_ge(s_dve, dve_stn)
                    if prev_tr:
                        scalar.wait_ge(s_pe, prev_tr)
                    scalar.activation(out=xtb[:], in_=xta[:], func=AF.Identity,
                                      bias=stn[0:128, 0:1], scale=strs[0:128, 0:1])
                    scalar.drain().then_inc(s_act, 1)
                act_xn = inc("act", 1)
                V["p0_prev_xn"] = act_xn
                emit("scalar", fa2)

                def fv3(vector, act_xn=act_xn):
                    vector.wait_ge(s_act, act_xn)
                    vector.scalar_tensor_tensor(out=xta[:], in0=xtb[:], scalar=2.0,
                                                in1=xta[:], op0=AluOpType.mult,
                                                op1=AluOpType.add)
                    vector.drain().then_inc(s_dve, 1)
                dve_r = inc("dve", 1)
                emit("vector", fv3)

                def fp(gpsimd, row0=row0, dve_r=dve_r):
                    gpsimd.wait_ge(s_dve, dve_r)
                    gpsimd.dma_start(out=r_scr[row0:row0 + 128, :], in_=xta[:]).then_inc(s_sr, 16)
                V["last_r_store"] = inc("sr", 16)
                emit("gpsimd", fp)
                V[f"r_store_{h}"] = V["last_r_store"]

                for icb in range(8):
                    pT_guard = V.get("pT_guard", 0)
                    ld_misc = V["ld_misc"]
                    def ft(tensor, icb=icb, act_xn=act_xn, pT_guard=pT_guard, ld_misc=ld_misc):
                        tensor.wait_ge(s_act, act_xn)
                        tensor.wait_ge(s_lm, ld_misc)
                        if pT_guard:
                            tensor.wait_ge(s_act, pT_guard)
                        for j in range(4):
                            ic = icb * 4 + j
                            tensor.transpose(bass.AP(pT, j * 128, [[512, 128], [1, 128]]),
                                             xtb[:, ic * 128:(ic + 1) * 128],
                                             ident[:]).then_inc(s_pe, 1)
                    pe_tr = inc("pe", 4)
                    emit("tensor", ft)

                    need_pe_guard = pe_guard if not first_evac_done[0] else 0
                    first_evac_done[0] = True
                    def fa3(scalar, icb=icb, st=st, pe_tr=pe_tr, need_pe_guard=need_pe_guard):
                        scalar.wait_ge(s_pe, pe_tr)
                        if need_pe_guard:
                            scalar.wait_ge(s_pe, need_pe_guard)
                        dst = bass.AP(big, (icb * 4) * 512 + st * 128,
                                      [[16384, 128], [512, 4], [1, 128]])
                        scalar.copy(out=dst, in_=pT[:])
                        scalar.drain().then_inc(s_act, 1)
                    V["pT_guard"] = inc("act", 1)
                    emit("scalar", fa3)
                V["p0_prev_tr"] = cur("pe")
            V[f"xnT_done_{h}"] = cur("act")

        def tgu(h):
            xnT_done = V[f"xnT_done_{h}"]
            ld_misc = V["ld_misc"]
            for name, src, p0_, dstT in (("g", gas, 0, tgT), ("u", uas, 32, tuT)):
                prev_evac = V.get(f"t_evac_{name}", 0)
                for ic in range(NIC):
                    def fm(tensor, src=src, ic=ic, p0_=p0_, xnT_done=xnT_done,
                           ld_misc=ld_misc, prev_evac=prev_evac):
                        if ic == 0:
                            tensor.wait_ge(s_act, xnT_done)
                            tensor.wait_ge(s_lm, ld_misc)
                            if prev_evac:
                                tensor.wait_ge(s_act, prev_evac)
                        ins = tensor.matmul(pt3[p0_:p0_ + 16, :],
                                            bass.AP(src, ic * 16, [[NIC * 16, 128], [1, 16]]),
                                            bigv(ic),
                                            start=(ic == 0), stop=(ic == NIC - 1))
                        if ic == NIC - 1:
                            ins.then_inc(s_pe, 1)
                    emit("tensor", fm)
                pe_t = inc("pe", 1)
                def fe(scalar, pe_t=pe_t, dstT=dstT, p0_=p0_):
                    scalar.wait_ge(s_pe, pe_t)
                    scalar.copy(out=dstT[:], in_=pt3[p0_:p0_ + 16, :])
                    scalar.drain().then_inc(s_act, 1)
                V[f"t_evac_{name}"] = inc("act", 1)
                emit("scalar", fe)
            V[f"t_ready_{h}"] = cur("act")

        def dq_tile(idx):
            mat, kc, ooff, ow = DQ_TILES[idx]
            slot = idx % 2
            segs = [(so, min(512, ow - so)) for so in range(0, ow, 512)]
            nob = ow // 128

            prev_s15 = V.get(f"dq_s15_{idx - 2}", 0)
            prev_bc = V.get(f"dq_bc_{idx - 2}", 0)
            def fl(sync, mat=mat, kc=kc, ooff=ooff, ow=ow, slot=slot, prev_s15=prev_s15):
                if prev_s15:
                    sync.wait_ge(s_act, prev_s15)
                sync.dma_start_transpose(
                    out=bass.AP(cbuf, slot * 2048, [[2 * 2048, 128], [1, ow]]),
                    in_=cod_src[mat][ooff:ooff + ow, kc * 128:(kc + 1) * 128]).then_inc(s_ldc[slot], 16)
            ldc_t = inc(f"ldc{slot}", 16)
            emit("sync", fl)

            def flam(gpsimd, mat=mat, kc=kc, ooff=ooff, ow=ow, slot=slot):
                gpsimd.dma_start(out=bass.AP(amb, slot * 2048, [[2 * 2048, 128], [1, ow]]),
                                 in_=am_src[mat][kc, :, ooff:ooff + ow]).then_inc(s_lam[slot], 16)
            lam_t = inc(f"lam{slot}", 16)
            emit("gpsimd", flam)

            for v in range(1, 16):
                zs = (v - 1) % 3
                def fs(scalar, v=v, zs=zs, ow=ow, slot=slot, idx=idx, ldc_t=ldc_t):
                    if v == 1:
                        scalar.wait_ge(s_ldc[slot], ldc_t)
                    if v >= 4:
                        scalar.wait_ge(s_dve, V[f"dq_sv_{idx}_{v - 3}"])
                    elif idx >= 1:
                        scalar.wait_ge(s_dve, V[f"dq_sv_{idx - 1}_{12 + v}"])
                    ins = scalar.activation(out=bass.AP(zbuf, zs * 2048, [[3 * 2048, 128], [1, ow]]),
                                            in_=bass.AP(cbuf, slot * 2048, [[2 * 2048, 128], [1, ow]]),
                                            func=AF.Sign, bias=kconst[0:128, v:v + 1],
                                            scale=1.0)
                    if v == 15:
                        scalar.drain().then_inc(s_act, 1)
                    else:
                        ins.then_inc(s_act, 1)
                V[f"dq_sg_{idx}_{v}"] = inc("act", 1)
                emit("scalar", fs)
            V[f"dq_s15_{idx}"] = V[f"dq_sg_{idx}_15"]

            prev_store = V.get(f"dq_st_{idx - 2}", 0)
            for v in range(1, 16):
                zs = (v - 1) % 3
                def fv(vector, v=v, zs=zs, ow=ow, slot=slot, idx=idx, prev_store=prev_store):
                    vector.wait_ge(s_act, V[f"dq_sg_{idx}_{min(v + 1, 15)}"])
                    if v == 1 and prev_store:
                        vector.wait_ge(s_stw[slot], prev_store)
                    zin = bass.AP(zbuf, zs * 2048, [[3 * 2048, 128], [1, ow]])
                    wv = bass.AP(wacc, slot * 2048, [[2 * 2048, 128], [1, ow]])
                    if v == 1:
                        vector.scalar_tensor_tensor(out=wv, in0=zin, scalar=HD[0],
                                                    in1=bass.AP(kconst, 16, [[17, 128], [0, ow]]),
                                                    op0=AluOpType.mult,
                                                    op1=AluOpType.add).then_inc(s_dve, 1)
                    elif v < 15:
                        vector.scalar_tensor_tensor(out=wv, in0=zin, scalar=HD[v - 1],
                                                    in1=wv, op0=AluOpType.mult,
                                                    op1=AluOpType.add).then_inc(s_dve, 1)
                    else:
                        vector.scalar_tensor_tensor(out=wv, in0=zin, scalar=HD[v - 1],
                                                    in1=wv, op0=AluOpType.mult,
                                                    op1=AluOpType.add)
                        vector.drain().then_inc(s_dve, 1)
                V[f"dq_sv_{idx}_{v}"] = inc("dve", 1)
                emit("vector", fv)

            stt15 = V[f"dq_sv_{idx}_15"]
            def fpm(gpsimd, mat=mat, kc=kc, ooff=ooff, ow=ow, slot=slot, nob=nob,
                    stt15=stt15, lam_t=lam_t):
                gpsimd.wait_ge(s_dve, stt15)
                gpsimd.wait_ge(s_lam[slot], lam_t)
                wv = bass.AP(wacc, slot * 2048, [[2 * 2048, 128], [1, ow]])
                gpsimd.tensor_tensor(out=wv, in0=wv,
                                     in1=bass.AP(amb, slot * 2048, [[2 * 2048, 128], [1, ow]]),
                                     op=AluOpType.mult)
                gpsimd.drain().then_inc(s_pl, 1)
                src = bass.AP(wacc, slot * 2048, [[2 * 2048, 128], [128, nob], [1, 128]])
                if mat in ("g", "u"):
                    dst = wt_dst[mat][ooff // 128: ooff // 128 + nob, kc]
                else:
                    dst = wt_dst["d"][kc, ooff // 128: ooff // 128 + nob]
                gpsimd.dma_start(out=dst.transpose([1, 0, 2]), in_=src).then_inc(s_stw[slot], 16)
            V[f"dq_pl_{idx}"] = inc("pl", 1)
            V[f"dq_st_{idx}"] = inc(f"stw{slot}", 16)
            emit("gpsimd", fpm)

        def w_store_marker():
            c0, c1 = cur("stw0"), cur("stw1")
            def fmk(gpsimd, c0=c0, c1=c1):
                if c0:
                    gpsimd.wait_ge(s_stw[0], c0)
                if c1:
                    gpsimd.wait_ge(s_stw[1], c1)
                gpsimd.sem_inc(s_stb, 1)
            emit("gpsimd", fmk)
            return inc("stb", 1)

        def x3_store_marker():
            c0, c1 = cur("sx0"), cur("sx1")
            def fmk(gpsimd, c0=c0, c1=c1):
                if c0:
                    gpsimd.wait_ge(s_sx[0], c0)
                if c1:
                    gpsimd.wait_ge(s_sx[1], c1)
                gpsimd.sem_inc(s_stb, 1)
            emit("gpsimd", fmk)
            return inc("stb", 1)

        def p1_ob(h, ob):
            st_need = V.get(f"dq_oc_{ob // 16}", 0)
            if ob > 0:
                g_prev = V[f"p1_peg_{h}_{ob - 1}"]
                u_prev = V[f"p1_pe_{h}_{ob - 1}"]
            else:
                g_prev = u_prev = V.get(f"p1_pe_{1 - h}_{NFC - 1}", 0)
            lane_sem = {"g": s_ldg, "u": s_ldw}
            lane_cnt = {"g": "ldg", "u": "ldw"}
            lane_pe = {"g": g_prev, "u": u_prev}
            lane_lb = {"g": (gbs, gb_h), "u": (ubs, ub_h)}
            lane_tot = {}
            for mi, mat in enumerate(("g", "u")):
                sem = lane_sem[mat]
                pep = lane_pe[mat]
                lb_s, lb_h = lane_lb[mat]
                def flw(sync, mat=mat, mi=mi, ob=ob, st_need=st_need, pep=pep,
                        sem=sem, lb_s=lb_s, lb_h=lb_h):
                    if mi == 0:
                        sync.wait_ge(s_stb, st_need)
                    if pep:
                        sync.wait_ge(s_pe, pep)
                    for grp in range(2):
                        sync.dma_start(out=bass.AP(wring, mi * 4096 + grp * 2048,
                                                   [[2 * 4096, 128], [128, 16], [1, 128]]),
                                       in_=wt_dst[mat][ob, grp * 16:(grp + 1) * 16]
                                       .transpose([1, 0, 2])).then_inc(sem, 16)
                    sync.dma_start(out=bass.AP(lb_s, (ob % 2) * 128, [[256, 16], [1, 128]]),
                                   in_=lb_h[:, ob * 128:(ob + 1) * 128]).then_inc(sem, 16)
                lane_tot[mat] = inc(lane_cnt[mat], 48)
                emit("sync", flw)

            pya = pA if ob % 2 == 0 else pC
            pyb = pB if ob % 2 == 0 else pD
            t_ready = V[f"t_ready_{h}"]
            x3_guard = V.get(f"x3_dve_{h}_{ob - 2}", 0)
            ldg_t, ldw_t = lane_tot["g"], lane_tot["u"]
            def fmm(tensor, h=h, ob=ob, pya=pya, pyb=pyb, ldg_t=ldg_t, ldw_t=ldw_t,
                    t_ready=t_ready, x3_guard=x3_guard):
                if ob == 0:
                    tensor.wait_ge(s_act, t_ready)
                if x3_guard:
                    tensor.wait_ge(s_dve, x3_guard)
                for mi, (ppp, tT, lsl, sem, tot) in enumerate((
                        (pya, tgT, gbs, s_ldg, ldg_t), (pyb, tuT, ubs, s_ldw, ldw_t))):
                    tensor.wait_ge(sem, tot)
                    for ic in range(NIC):
                        tensor.matmul(ppp[:],
                                      bass.AP(wring, mi * 4096 + ic * 128,
                                              [[2 * 4096, 128], [1, 128]]),
                                      bigv(ic), start=(ic == 0), stop=False)
                    tensor.matmul(ppp[:],
                                  bass.AP(lsl, (ob % 2) * 128, [[256, 16], [1, 128]]),
                                  tT[:], start=False, stop=True).then_inc(s_pe, 1)
            pe_y = inc("pe", 2)
            V[f"p1_peg_{h}_{ob}"] = pe_y - 1
            V[f"p1_pe_{h}_{ob}"] = pe_y
            emit("tensor", fmm)

            def fre(scalar, pe_y=pe_y, ob=ob, pya=pya, x3_guard=x3_guard):
                scalar.wait_ge(s_pe, pe_y)
                if x3_guard:
                    scalar.wait_ge(s_dve, x3_guard)
                scalar.activation(out=bass.AP(x3t, (ob % 2) * 512, [[1024, 128], [1, 512]]),
                                  in_=pya[:], func=AF.Relu, bias=0.0, scale=1.0)
                scalar.drain().then_inc(s_act, 1)
            act_re = inc("act", 1)
            emit("scalar", fre)

            st_guard = V.get(f"x3_st_{h}_{ob - 2}", 0)
            if not st_guard and ob < 2:
                st_guard = V.get(f"x3_st_{1 - h}_{NFC - 2 + ob}", 0)
            def fx3(vector, act_re=act_re, ob=ob, pyb=pyb, st_guard=st_guard):
                vector.wait_ge(s_act, act_re)
                if st_guard:
                    vector.wait_ge(s_sx[ob % 2], st_guard)
                vector.tensor_tensor(out=bass.AP(x3t, (ob % 2) * 512, [[1024, 128], [1, 512]]),
                                     in0=bass.AP(x3t, (ob % 2) * 512, [[1024, 128], [1, 512]]),
                                     in1=pyb[:], op=AluOpType.mult)
                vector.drain().then_inc(s_dve, 1)
            dve_x3 = inc("dve", 1)
            V[f"x3_dve_{h}_{ob}"] = dve_x3
            emit("vector", fx3)

            td_guard = V.get("td_evac_0", 0) if (h == 1 and ob == 0) else 0
            def ftd(tensor, ob=ob, h=h, dve_x3=dve_x3, td_guard=td_guard):
                tensor.wait_ge(s_dve, dve_x3)
                if td_guard:
                    tensor.wait_ge(s_act, td_guard)
                tensor.matmul(pt3[64:80, :],
                              bass.AP(das, ob * 16, [[NFC * 16, 128], [1, 16]]),
                              bass.AP(x3t, (ob % 2) * 512, [[1024, 128], [1, 512]]),
                              start=(ob == 0), stop=(ob == NFC - 1)).then_inc(s_pe, 1)
            pe_td = inc("pe", 1)
            emit("tensor", ftd)
            if ob == NFC - 1:
                def fetd(scalar, pe_td=pe_td, h=h):
                    scalar.wait_ge(s_pe, pe_td)
                    scalar.copy(out=tdT[:, h * 512:(h + 1) * 512],
                                in_=pt3[64:80, :])
                    scalar.drain().then_inc(s_act, 1)
                V[f"td_evac_{h}"] = inc("act", 1)
                emit("scalar", fetd)

            def fsx(gpsimd, ob=ob, h=h, dve_x3=dve_x3):
                gpsimd.wait_ge(s_dve, dve_x3)
                gpsimd.dma_start(out=x3_scr[ob, :, h * 512:(h + 1) * 512],
                                 in_=bass.AP(x3t, (ob % 2) * 512, [[1024, 128], [1, 512]])).then_inc(s_sx[ob % 2], 16)
            V[f"x3_st_{h}_{ob}"] = inc(f"sx{ob % 2}", 16)
            emit("gpsimd", fsx)

        def p2(h):
            pe_before = cur("pe")
            out_prev = V.get("out_store", 0)
            r_done = V[f"r_store_{h}"]
            r_all = cur("sr")
            def fri(sync, h=h, pe_before=pe_before, out_prev=out_prev, r_all=r_all):
                sync.wait_ge(s_pe, pe_before)
                if out_prev:
                    sync.wait_ge(s_so, out_prev)
                sync.wait_ge(s_sr, r_all)
                sync.dma_start(out=bass.AP(big, 0, [[16384, 128], [4096, 4], [1, 4096]]),
                               in_=r_scr.rearrange("(hh sb p) d -> hh p sb d", hh=2, p=128)[h]
                               ).then_inc(s_lr, 16)
            ldr = inc("lr", 16)
            emit("sync", fri)

            x3_all = V[f"x3_done_{h}"]
            bank_guard = {0: 0, 1: 0}
            wd_guard = {0: V.get("p2_wd_free", 0), 1: V.get("p2_wd_free", 0)}
            for g in range(NG):
                fcs = list(range(g * 8, min(g * 8 + 8, NFC)))
                buf = xta if g % 2 == 0 else xtb
                grp_guard = V.get(f"p2_pe_{h}_{g - 2}", 0) or (pe_before if g < 2 else 0)
                gp = g % 2
                def flx(scalar, fcs=fcs, buf=buf, h=h, gp=gp, grp_guard=grp_guard, x3_all=x3_all):
                    if grp_guard:
                        scalar.wait_ge(s_pe, grp_guard)
                    scalar.wait_ge(s_stb, x3_all)
                    for k, fc in enumerate(fcs):
                        scalar.dma_start(out=bass.AP(buf, k * 512, [[4096, 128], [1, 512]]),
                                         in_=x3_scr[fc, :, h * 512:(h + 1) * 512]).then_inc(s_lg[gp], 16)
                inc(f"lg{gp}", 16 * len(fcs))
                emit("scalar", flx)
                ldx_grp = cur(f"lg{gp}")

                for ot in range(NOT):
                    otp = ot % 2
                    wq = "sync" if otp == 0 else "gpsimd"
                    wsem = s_ldw if otp == 0 else s_ldg
                    wcnt = "ldw" if otp == 0 else "ldg"
                    wg = wd_guard[otp]
                    dmark = V.get("dq_d_mark", 0)
                    last_g = g == NG - 1
                    def flwd(eng, fcs=fcs, ot=ot, otp=otp, h=h, wg=wg, dmark=dmark,
                             wsem=wsem, last_g=last_g):
                        if wg:
                            eng.wait_ge(s_pe, wg)
                        if dmark:
                            eng.wait_ge(s_stb, dmark)
                        for k, fc in enumerate(fcs):
                            eng.dma_start(out=bass.AP(wring, otp * 4096 + k * 512,
                                                      [[2 * 4096, 128], [128, 4], [1, 128]]),
                                          in_=wt_dst["d"][fc, ot * 4:(ot + 1) * 4]
                                          .transpose([1, 0, 2])).then_inc(wsem, 16)
                        if last_g:
                            eng.dma_start(out=bass.AP(dbs, otp * 512, [[1024, 16], [1, 512]]),
                                          in_=db_h[:, ot * 512:(ot + 1) * 512]).then_inc(wsem, 16)
                    inc(wcnt, 16 * (len(fcs) + (1 if last_g else 0)))
                    emit(wq, flwd)
                    ldw_last = cur(wcnt)

                    for sb in range(4):
                        par = (ot * 4 + sb) % 2
                        pp = pA if par == 0 else pB
                        bg = bank_guard[par]
                        td_ev = V.get("td_evac_1", 0)
                        def fm2(tensor, g=g, ot=ot, otp=otp, sb=sb, fcs=fcs, buf=buf, pp=pp, h=h,
                                ldx_grp=ldx_grp, ldw_last=ldw_last, bg=bg, td_ev=td_ev,
                                wsem=wsem):
                            if sb == 0:
                                tensor.wait_ge(s_lg[g % 2], ldx_grp)
                                tensor.wait_ge(wsem, ldw_last)
                            if bg:
                                tensor.wait_ge(s_dve, bg)
                            last_g = g == NG - 1
                            for k in range(len(fcs)):
                                ins = tensor.matmul(pp[:],
                                                    bass.AP(buf, k * 512 + sb * 128,
                                                            [[4096, 128], [1, 128]]),
                                                    bass.AP(wring, otp * 4096 + k * 512,
                                                            [[2 * 4096, 128], [1, 512]]),
                                                    start=(k == 0),
                                                    stop=(not last_g and k == len(fcs) - 1))
                            if last_g:
                                tensor.wait_ge(s_act, td_ev)
                                ins = tensor.matmul(pp[:],
                                                    bass.AP(tdT, h * 512 + sb * 128, [[1024, 16], [1, 128]]),
                                                    bass.AP(dbs, otp * 512, [[1024, 16], [1, 512]]),
                                                    start=False, stop=True)
                            ins.then_inc(s_pe, 1)
                        pe_mm2 = inc("pe", 1)
                        emit("tensor", fm2)

                        def fad(vector, sb=sb, ot=ot, pp=pp, pe_mm2=pe_mm2, g=g, ldr=ldr):
                            vector.wait_ge(s_pe, pe_mm2)
                            if g == 0 and ot == 0 and sb < 2:
                                vector.wait_ge(s_lr, ldr)
                            dst = accv(sb, ot * 512, 512)
                            vector.tensor_tensor(out=dst, in0=dst, in1=pp[:],
                                                 op=AluOpType.add)
                            vector.drain().then_inc(s_dve, 1)
                        bank_guard[par] = inc("dve", 1)
                        emit("vector", fad)
                    wd_guard[otp] = cur("pe")
                V[f"p2_pe_{h}_{g}"] = cur("pe")
            V["p2_wd_free"] = cur("pe")
            dve_fin = cur("dve")

            for sb in range(4):
                def fso(gpsimd, sb=sb, h=h, dve_fin=dve_fin):
                    gpsimd.wait_ge(s_dve, dve_fin)
                    gpsimd.dma_start(out=out_h[h * 512 + sb * 128: h * 512 + (sb + 1) * 128, :],
                                     in_=accv(sb, 0, 4096)).then_inc(s_so, 16)
                inc("so", 16)
                emit("gpsimd", fso)
            V["out_store"] = cur("so")

        p0(0)
        if phases == "p0":
            p0(1)
        if phases in ("p1h0", "full"):
            tgu(0)
            ob_done = 0
            pending = []
            for oci, (oo, ow_) in enumerate(G_OCS):
                base = oci * 2 * NIC
                ti, k = 0, 0
                while ti < 2 * NIC or k < len(pending):
                    for _ in range(4):
                        if ti < 2 * NIC:
                            dq_tile(base + ti)
                            ti += 1
                    if k < len(pending):
                        p1_ob(0, pending[k])
                        k += 1
                V[f"dq_oc_{oci}"] = w_store_marker()
                nob_end = (oo + ow_) // 128
                pending = list(range(ob_done, nob_end))
                ob_done = nob_end
            for ob in pending:
                p1_ob(0, ob)
            V["x3_done_0"] = x3_store_marker()

        if phases == "full":
            p0(1)
            tgu(1)
            di = 0
            for ob in range(NFC):
                for _ in range(2):
                    if di < len(D_TILES):
                        dq_tile(len(GU_TILES) + di)
                        di += 1
                p1_ob(1, ob)
            while di < len(D_TILES):
                dq_tile(len(GU_TILES) + di)
                di += 1
            V["dq_d_mark"] = w_store_marker()
            V["x3_done_1"] = x3_store_marker()

            p2(0)
            p2(1)

        if debug:
            dve_now = cur("dve")
            ndump = 1 if phases.startswith("p0") else 5
            def fdbg(gpsimd, dve_now=dve_now, ndump=ndump):
                gpsimd.wait_ge(s_dve, dve_now)
                gpsimd.dma_start(out=dbg_r[:], in_=r_scr[:]).then_inc(s_st, 16)
                if ndump > 1:
                    for fc in range(4):
                        gpsimd.dma_start(out=dbg_x3[fc], in_=x3_scr[fc]).then_inc(s_st, 16)
                    wsel = [("g", 0, 0), ("g", 0, 31), ("g", 50, 0), ("g", 85, 31),
                            ("u", 0, 0), ("u", 85, 31), ("d", 0, 0), ("d", 85, 31)]
                    for wi, (wm, wa, wb) in enumerate(wsel):
                        gpsimd.dma_start(out=dbg_w[wi], in_=wt_dst[wm][wa, wb]).then_inc(s_st, 16)
            inc("st", 16 * (ndump + 8 if ndump > 1 else ndump))
            emit("gpsimd", fdbg)

        so_final = cur("so")
        st_final = cur("st")
        def ffin(gpsimd, so_final=so_final, st_final=st_final):
            if so_final:
                gpsimd.wait_ge(s_so, so_final)
            if st_final:
                gpsimd.wait_ge(s_st, st_final)
        emit("gpsimd", ffin)

        @blk.sync
        def _(sync):
            for f in ops["sync"]:
                f(sync)

        @blk.tensor
        def _(tensor):
            for f in ops["tensor"]:
                f(tensor)

        @blk.vector
        def _(vector):
            for f in ops["vector"]:
                f(vector)

        @blk.scalar
        def _(scalar):
            for f in ops["scalar"]:
                f(scalar)

        @blk.gpsimd
        def _(gpsimd):
            for f in ops["gpsimd"]:
                f(gpsimd)

    return nc


def kernel(L0_attn_x, L0_ffn_g_codes, L0_ffn_g_absmax, L0_ffn_g_lora_a, L0_ffn_g_lora_b,
           L0_ffn_u_codes, L0_ffn_u_absmax, L0_ffn_u_lora_a, L0_ffn_u_lora_b,
           L0_ffn_d_codes, L0_ffn_d_absmax, L0_ffn_d_lora_a, L0_ffn_d_lora_b):
    x = np.ascontiguousarray(np.asarray(L0_attn_x, dtype=np.float32).reshape(8192, 4096))

    def lo16(a):
        a = np.ascontiguousarray(np.asarray(a, dtype="<i4"))
        return np.ascontiguousarray(a.view("<u2")[:, 0::2])

    def _amx(am, nkc):
        amT = np.asarray(am, dtype=np.float32).T
        return np.ascontiguousarray(np.repeat(amT, 64, axis=0).reshape(nkc, 128, amT.shape[1]))

    common = {
        "g16": lo16(L0_ffn_g_codes), "u16": lo16(L0_ffn_u_codes), "d16": lo16(L0_ffn_d_codes),
        "amxg": _amx(L0_ffn_g_absmax, NIC),
        "amxu": _amx(L0_ffn_u_absmax, NIC),
        "amxd": _amx(L0_ffn_d_absmax, NFC),
        "ga": np.ascontiguousarray(np.asarray(L0_ffn_g_lora_a, dtype=np.float32)),
        "gb": np.ascontiguousarray(np.asarray(L0_ffn_g_lora_b, dtype=np.float32)),
        "ua": np.ascontiguousarray(np.asarray(L0_ffn_u_lora_a, dtype=np.float32)),
        "ub": np.ascontiguousarray(np.asarray(L0_ffn_u_lora_b, dtype=np.float32)),
        "da": np.ascontiguousarray(np.asarray(L0_ffn_d_lora_a, dtype=np.float32)),
        "db": np.ascontiguousarray(np.asarray(L0_ffn_d_lora_b, dtype=np.float32)),
        "ident": np.eye(128, dtype=np.float32),
    }
    nc = build_nc()
    in_maps = [dict(common, x=np.ascontiguousarray(x[k * 1024:(k + 1) * 1024]))
               for k in range(8)]
    res = run_bass_kernel_spmd(nc, in_maps, core_ids=list(range(8)))
    out = np.concatenate([np.asarray(r["out"]) for r in res.results], axis=0)
    return out.reshape(1, 8192, 4096)
